# version 12
# baseline (speedup 1.0000x reference)
import math
import numpy as np

import concourse.bass as bass
import concourse.mybir as mybir
from concourse.bass import AP
from concourse.bass_utils import run_bass_kernel_spmd

F32 = mybir.dt.float32
F32R = mybir.dt.float32r
AF = mybir.ActivationFunctionType
AL = mybir.AluOpType
AX = mybir.AxisListType

B, T, C = 2, 8192, 1024
NH, BD, HS, DELTA = 16, 16, 64, 32
DFF = 4096
NCORE = 8
TC = 2048
NCHUNK, CH = 4, 512
NJT, JT = 4, 128
HALO = 62
TCOL = HALO + CH
KC = C // 128
MT1 = DFF // 128
PP = 256
SROW = NH * PP
SREG = 97 + 128 * SROW
EPS = 1e-5
TWOPI, PI, HPI = 2.0 * math.pi, math.pi, math.pi / 2.0

_CACHE = {}


class _Op:
    __slots__ = ("engine", "fn", "deps", "needs_inc", "val", "sem_key")

    def __init__(self, engine, fn, always_inc, sem_key):
        self.engine = engine
        self.fn = fn
        self.deps = []
        self.needs_inc = always_inc
        self.val = None
        self.sem_key = sem_key


class Sched:

    DMA_ENGINES = ("sync", "gpsimd")

    def __init__(self):
        self.order = {e: [] for e in ("tensor", "vector", "scalar", "gpsimd", "sync")}
        self.seq = []
        self.last_w = {}
        self.readers = {}

    def op(self, engine, fn, reads=(), writes=(), always_inc=False, sem_key=None):
        if engine in self.DMA_ENGINES:
            if sem_key is None:
                pre = "dg:" if engine == "gpsimd" else "d:"
                sem_key = pre + (writes[0] if writes else "misc")
        else:
            sem_key = engine
        o = _Op(engine, fn, always_inc or engine in self.DMA_ENGINES, sem_key)
        deps = []
        for t in reads:
            p = self.last_w.get(t)
            if p is not None:
                deps.append(p)
        for t in writes:
            deps.extend(self.readers.get(t, ()))
            p = self.last_w.get(t)
            if p is not None:
                deps.append(p)
        for p in set(deps):
            if p is o:
                continue
            if p.engine == "tensor" and engine == "tensor":
                continue
            o.deps.append(p)
            p.needs_inc = True
        for t in reads:
            self.readers.setdefault(t, []).append(o)
        for t in writes:
            self.last_w[t] = o
            self.readers[t] = []
        self.order[engine].append(o)
        self.seq.append(o)
        return o

    def finalize(self, nc, total_keys=()):
        keys = []
        for e, lst in self.order.items():
            for o in lst:
                if o.sem_key not in keys:
                    keys.append(o.sem_key)
        sems = {k: nc.alloc_semaphore("s_" + k.replace(":", "_")) for k in keys}
        cnt = {k: 0 for k in keys}
        for o in self.seq:
            if o.needs_inc:
                cnt[o.sem_key] += 16 if o.engine in self.DMA_ENGINES else 1
                o.val = cnt[o.sem_key]
        for e, lst in self.order.items():
            for o in lst:
                if o.sem_key in total_keys and o.val is not None:
                    o.val = cnt[o.sem_key]
        self.emitted = {}
        nwaits = 0
        for e, lst in self.order.items():
            step = 16 if e in self.DMA_ENGINES else 1
            sat = {}
            out = []
            for o in lst:
                waits = {}
                for p in o.deps:
                    assert p.val is not None, "dep without inc"
                    if waits.get(p.sem_key, 0) < p.val:
                        waits[p.sem_key] = p.val
                wl = []
                for s, v in waits.items():
                    if sat.get(s, 0) >= v:
                        continue
                    wl.append((sems[s], v))
                    sat[s] = v
                nwaits += len(wl)
                out.append((tuple(wl), o.fn,
                            sems[o.sem_key] if o.needs_inc else None, step))
            self.emitted[e] = out
        self.stats = {e: len(l) for e, l in self.order.items()}
        self.stats["waits"] = nwaits
        self.stats["sems"] = len(keys)

    def emit(self, engine, eng):
        for waits, fn, sem, inck in self.emitted[engine]:
            for s, v in waits:
                eng.wait_ge(s, v)
            bi = fn(eng)
            if sem is not None and bi is not None:
                bi.then_inc(sem, inck)


def build_nc():
    if "nc" in _CACHE:
        return _CACHE["nc"]
    nc = bass.Bass("TRN2", target_bir_lowering=False)

    din = {}

    def DI(name, shape):
        din[name] = nc.dram_tensor(name, list(shape), F32, kind="ExternalInput")
        return din[name]

    xT = DI("xT", (C, 128 + TC))
    w1re = DI("w1re", (MT1, 128, KC * 128))
    w2 = DI("w2", (DFF, C))
    wval = DI("wval", (C, C))
    wcpre = DI("wcpre", (KC, 128, KC * 128))
    wdispT = DI("wdispT", (128, KC * 256))
    for nm in ("bd_state", "bd_strain", "bd_damage", "bd_si", "bd_sj"):
        DI(nm, (128, 2 * 256))
    DI("master_bond", (128, 143))
    DI("master_dmg", (128, 143))
    DI("relpeT", (BD, DELTA))
    DI("wpos", (BD, BD))
    DI("theta_t", (128, 32))
    DI("wreal_t", (128, 32))
    DI("wimag_t", (128, 32))
    DI("b1_t", (128, 32))
    DI("b2_t", (128, 8))
    DI("bdisp_t", (128, 2))
    DI("bval_rep", (128, C))
    DI("bcproj_t", (128, 8))
    DI("bstate_rep", (128, 1))
    DI("bstrain_rep", (128, 1))
    DI("bdmg_rep", (128, 1))
    DI("bdmgout_rep", (128, 1))
    DI("iden", (128, 128))
    DI("ones_col", (128, 1))
    DI("hpi_col", (128, 1))
    DI("invc", (NCHUNK, 128, TCOL - 31))
    DI("qmask", (128, HALO))
    DI("expmask", (128, 62))
    out_d = nc.dram_tensor("out", [C, TC], F32, kind="ExternalOutput")
    scratch = nc.dram_tensor("scratch", [2 * SREG], F32, kind="Internal")
    rowscr = nc.dram_tensor("rowscr", [8 * 1024], F32, kind="Internal")

    sb = {}

    def SA(name, cols, parts=128):
        sb[name] = nc.alloc_sbuf_tensor(name, [parts, cols], F32)
        return sb[name].ap()

    x_b = SA("x_b", KC * CH)
    x1_b = SA("x1_b", KC * CH)
    val_b = SA("val_b", 5 * C)
    q_b = SA("q_b", 2 * TCOL)
    p_b = SA("p_b", 2 * TCOL)
    r_b = SA("r_b", 2 * TCOL)
    st_b = SA("st_b", 2 * TCOL)
    bv_b = SA("bv_b", 2 * TCOL)
    wbufA = SA("wbufA", 2 * KC * 128)
    wbufB = SA("wbufB", 2 * C)
    mast_b = SA("mast_b", 143)
    mast_d = SA("mast_d", 143)
    iden_s = SA("iden_s", 128)
    ones_col = SA("ones_col_s", 1)
    ones_col_r = nc.alloc_sbuf_tensor("ones_col_r", [128, 1], F32R).ap()
    cc1 = SA("cc1", 32)
    cs1 = SA("cs1", 32)
    b1s = SA("b1s", 32)
    bdisps = SA("bdisps", 2)
    bvalr = SA("bvalr", C)
    bstater = SA("bstater", 1)
    bstrainr = SA("bstrainr", 1)
    bdmgr = SA("bdmgr", 1)
    bdmgor = SA("bdmgor", 1)
    hpi_col = SA("hpi_col_s", 1)
    e_rep = SA("e_rep", 32)
    qmask_s = SA("qmask_s", HALO)
    expmask_s = SA("expmask_s", 62)
    tmp32 = SA("tmp32", 32)
    tmp32b = SA("tmp32b", 32)
    sc32 = SA("sc32", 32)
    msk32 = SA("msk32", 32)
    wpos_s = SA("wpos_s", 16, parts=16)
    relpe_s = SA("relpe_s", 32, parts=16)
    rowpool = SA("rowpool", 8064)
    R_MEAN, R_NEGM, R_A, R_TMP = 0, CH, 2 * CH, 3 * CH
    CSW1_O, CSVAL_O, CSDISP_O = 4 * CH, 4 * CH + DFF, 4 * CH + DFF + C
    R_MA, R_ONES = CSDISP_O + 256, CSDISP_O + 256 + CH

    def rrow(off, n):
        return rowpool[0:1, off:off + n]

    def r32(off, n):
        return rowpool[32:33, off:off + n]
    a2col = SA("a2col", NJT + 1)
    sums = SA("sums", 48)
    ma_bc = SA("ma_bc", CH)
    b2col = SA("b2col", 8)
    bcpcol = SA("bcpcol", 8)
    csw1c = SA("csw1c", 32)
    csdispc = SA("csdispc", 2)

    ARENA = 18432
    arena = SA("arena", ARENA)
    arena_r = nc.alloc_sbuf_tensor_at(
        "arena_r", [128, ARENA], F32R,
        offset=nc.lookup_mloc(sb["arena"]).addr).ap()

    def ar_r(off, n):
        return arena_r[:, off:off + n]
    O_PRE_B = 0
    O_PRE_D = 2048
    O_ACT_B = 4096
    O_ACT_D = 6144
    O_DISP = 8192
    O_SQ = 9344
    O_SIG = 8192
    O_ESTK = 10496
    O_WSH = 12544
    O_WBL = 13056
    O_ATT = 13568
    O_SCR = 16384
    O_STG = 15616
    O_WTM = 0
    O_WTM1 = 5120
    O_WSHX = 4096
    O_STG4 = 14336

    def h_t(i):
        return arena[:, i * CH:(i + 1) * CH]

    def ar(off, n):
        return arena[:, off:off + n]

    def artok(off, n):
        return ["ar:%d" % b for b in range(off // 512, (off + n + 511) // 512)]

    wtm2 = [ar(O_WTM, SROW), ar(O_WTM1, SROW)]
    WTM_T2 = [artok(O_WTM, SROW), artok(O_WTM1, SROW)]
    a_bc = SA("a_bc", CH)
    ABC_T = ["a_bc"]
    stg = ar(O_STG, 2 * CH)
    disp_b = ar(O_DISP, 2 * TCOL)
    DISP_T = artok(O_DISP, 2 * TCOL)
    sq_b = ar(O_SQ, 2 * TCOL)
    SQ_T = artok(O_SQ, 2 * TCOL)
    invc_b = ar(O_ESTK, TCOL - 31)
    INVC_T = artok(O_ESTK, TCOL - 31)

    def xh_of(kk):
        return x_b[:, kk * 128:(kk + 1) * 128]

    def x1h_of(kk):
        return x1_b[:, kk * 128:(kk + 1) * 128]

    def XT(kk):
        return "x%d" % kk

    def X1T(kk):
        return "x1_%d" % kk

    XALL = [XT(k) for k in range(KC)]
    X1ALL = [X1T(k) for k in range(KC)]

    ps = [nc.alloc_psum_tensor("ps%d" % i, [128, 512], F32).ap() for i in range(8)]
    PT = ["ps%d" % i for i in range(8)]

    S = Sched()

    def dma(engine, dst, src, reads=(), writes=(), always_inc=False, sem_key=None,
            slow=False):
        return S.op(engine,
                    lambda e, d_=dst, s_=src, sl=slow: e.dma_start(
                        d_, s_, allow_slow_non_contiguous=sl),
                    reads=reads, writes=writes, always_inc=always_inc,
                    sem_key=sem_key)

    def pe_bcast(row_ap, n, dst_ap, psb, reads, writes):
        top(lambda e, row_ap=row_ap, n=n, psb=psb: e.matmul(
            ps[psb][:, 0:n], rrow(R_ONES, 128), row_ap, start=True, stop=True),
            reads=["rp_ones"] + list(reads), writes=[PT[psb]])
        vop(lambda e, dst_ap=dst_ap, n=n, psb=psb: e.tensor_copy(
            dst_ap, ps[psb][:, 0:n]),
            reads=[PT[psb]], writes=writes)

    _bounce_ctr = [0]

    def bcast_row(row_ap, n, dst_ap, reads, writes):
        slot = _bounce_ctr[0] % 4
        _bounce_ctr[0] += 1
        off = slot * 1024
        tok = "rsc%d" % slot
        if n > 1024:
            off, tok = 4096, "rsc4"
        dma("gpsimd", AP(rowscr, off, [[1, 1], [1, n]]), row_ap,
            reads=reads, writes=[tok])
        dma("gpsimd", dst_ap,
            AP(rowscr, off, [[0, 128], [1, n]]), reads=[tok], writes=writes)

    def spread_row(row_ap, n, dst_ap, reads, writes):
        slot = _bounce_ctr[0] % 4
        _bounce_ctr[0] += 1
        off = slot * 1024
        tok = "rsc%d" % slot
        if n > 1024:
            off, tok = 4096, "rsc4"
        dma("gpsimd", AP(rowscr, off, [[1, 1], [1, n]]), row_ap,
            reads=reads, writes=[tok])
        if n <= 128:
            dma("gpsimd", dst_ap, AP(rowscr, off, [[1, n], [1, 1]]),
                reads=[tok], writes=writes)
        else:
            dma("gpsimd", dst_ap, AP(rowscr, off, [[1, 128], [128, n // 128]]),
                reads=[tok], writes=writes, slow=True)

    def top(fn, reads=(), writes=()):
        return S.op("tensor", fn, reads=reads, writes=writes)

    def vop(fn, reads=(), writes=()):
        return S.op("vector", fn, reads=reads, writes=writes)

    def aop(fn, reads=(), writes=()):
        return S.op("scalar", fn, reads=reads, writes=writes)

    def wbt(lo, n):
        return ["wBg%d" % g for g in range(lo // 512, (lo + n + 511) // 512)]

    def wAtok(buf, h):
        return "wA%d_%d" % (buf, h)

    def _load_wA(src, i):
        buf = i % 2
        wdst = wbufA[:, buf * KC * 128:(buf + 1) * KC * 128]
        dma("sync", wdst, src[i],
            writes=[wAtok(buf, q) for q in range(4)])

    def load_w1(m):
        _load_wA(w1re, m)

    def load_wcp(mo):
        _load_wA(wcpre, mo)

    for (dst, key, tok) in [
        (ones_col, "ones_col", "onesc"), (iden_s, "iden", "iden"),
        (b2col, "b2_t", "b2col"),
        (tmp32, "theta_t", "tmp32"), (cc1, "wreal_t", "cc1"),
        (cs1, "wimag_t", "cs1"), (b1s, "b1_t", "b1s"),
        (hpi_col, "hpi_col", "hpi"),
    ]:
        dma("sync", dst, din[key].ap(), writes=[tok], sem_key="d:cst0")
    for kk in range(KC):
        dma("sync", x_b[:, kk * 128:(kk + 1) * 128],
            xT[kk * 128:(kk + 1) * 128, 0:128], writes=[XT(kk // 4)])
    load_w1(0)
    load_w1(1)
    for (dst, key, tok) in [
        (bcpcol, "bcproj_t", "bcpcol"),
        (mast_b, "master_bond", "mastb"), (mast_d, "master_dmg", "mastd"),
        (bdisps, "bdisp_t", "bdisps"),
        (bvalr, "bval_rep", "bvalr"),
        (bstater, "bstate_rep", "bstater"), (bstrainr, "bstrain_rep", "bstrainr"),
        (bdmgr, "bdmg_rep", "bdmgr"), (bdmgor, "bdmgout_rep", "bdmgor"),
        (qmask_s, "qmask", "qmask"), (expmask_s, "expmask", "expmask"),
        (wpos_s, "wpos", "wpos"),
        (relpe_s, "relpeT", "relpe"),
    ]:
        dma("sync", dst, din[key].ap(), writes=[tok], sem_key="d:cst")

    vop(lambda e: e.tensor_copy(ones_col_r, ones_col), reads=["onesc"],
        writes=["onescr"])
    vop(lambda e: e.memset(rowpool[64:65, 0:128], 0.0), writes=["zrow"])
    vop(lambda e: e.memset(rrow(R_ONES, 128), 1.0), writes=["rp_ones"])
    vop(lambda e: e.memset(val_b[:, 0:C], 0.0), writes=["val0"])
    for reg in range(2):
        dma("gpsimd", AP(scratch, reg * SREG, [[1, 97]]),
            rowpool[64:65, 0:97], reads=["zrow"], writes=["sczero%d" % reg])

    def ln_stats(x_of_k, xtok, ncols):
        for kk in range(KC):
            xr = ar_r(O_SCR + 1024 + (kk % 2) * CH, ncols)
            xrt = artok(O_SCR + 1024 + (kk % 2) * CH, ncols)
            sqv = ar_r(O_SCR + (kk % 2) * CH, ncols)
            stk = artok(O_SCR + (kk % 2) * CH, ncols)
            vop(lambda e, xr=xr, kk=kk: e.tensor_copy(xr, x_of_k(kk)),
                reads=xtok(kk), writes=xrt)
            top(lambda e, xr=xr, kk=kk: e.matmul(ps[2][0:1, 0:ncols], ones_col_r, xr,
                                                 start=(kk == 0), stop=(kk == KC - 1)),
                reads=["onescr"] + xrt, writes=["ps2"])
            aop(lambda e, sqv=sqv, kk=kk: e.activation(sqv, x_of_k(kk), AF.Square),
                reads=xtok(kk), writes=stk)
            top(lambda e, sqv=sqv, kk=kk: e.matmul(ps[3][0:1, 0:ncols], ones_col_r,
                                                   sqv,
                                                   start=(kk == 0), stop=(kk == KC - 1)),
                reads=["onescr"] + stk, writes=["ps3"])
        rm = rrow(R_MEAN, ncols)
        rn = rrow(R_NEGM, ncols)
        ra = rrow(R_A, ncols)
        rt = rrow(R_TMP, ncols)
        vop(lambda e: e.tensor_scalar_mul(rm, ps[2][0:1, 0:ncols], 1.0 / C),
            reads=["ps2"], writes=["r_mean"])
        vop(lambda e: e.tensor_scalar_mul(rn, rm, -1.0), reads=["r_mean"],
            writes=["r_negm"])
        vop(lambda e: e.tensor_mul(rt, rm, rm), reads=["r_mean"], writes=["r_tmp"])
        vop(lambda e: e.scalar_tensor_tensor(rt, ps[3][0:1, 0:ncols], 1.0 / C, rt,
                                             AL.mult, AL.subtract),
            reads=["ps3", "r_tmp"], writes=["r_tmp"])
        vop(lambda e: e.tensor_scalar_add(rt, rt, EPS), reads=["r_tmp"],
            writes=["r_tmp"])
        aop(lambda e: e.activation(ra, rt, AF.Ln), reads=["r_tmp"], writes=["r_a"])
        aop(lambda e: e.activation(ra, ra, AF.Exp, scale=-0.5), reads=["r_a"],
            writes=["r_a"])
        vop(lambda e: e.tensor_mul(rm, rt, ra), reads=["r_tmp", "r_a"],
            writes=["r_mean"])
        vop(lambda e: e.tensor_mul(rm, rm, ra), reads=["r_mean", "r_a"],
            writes=["r_mean"])
        vop(lambda e: e.tensor_scalar(rm, rm, -0.5, 1.5, AL.mult, AL.add),
            reads=["r_mean"], writes=["r_mean"])
        vop(lambda e: e.tensor_mul(ra, ra, rm), reads=["r_a", "r_mean"],
            writes=["r_a"])
        vop(lambda e: e.tensor_mul(rrow(R_MA, ncols), rn, ra),
            reads=["r_negm", "r_a"], writes=["r_ma"])

    def ffn_pass(x_of_k, xtok, ncols, compute_cs, post_evac,
                 skip_w1=0, after_w1=None, skip_stats=False):
        if not skip_stats:
            ln_stats(x_of_k, xtok, ncols)
        for m in range(MT1):
            buf = m % 2
            wdst = wbufA[:, buf * KC * 128:(buf + 1) * KC * 128]
            if m >= skip_w1:
                load_w1(m)
            pb = ps[m % 2][:, 0:ncols]
            pt = PT[m % 2]
            for kk in range(KC):
                top(lambda e, kk=kk, pb=pb, wdst=wdst: e.matmul(
                    pb, wdst[:, kk * 128:(kk + 1) * 128], x_of_k(kk),
                    start=(kk == 0), stop=(kk == KC - 1 and not compute_cs)),
                    reads=[wAtok(buf, kk // 2)] + xtok(kk), writes=[pt])
            if m == 0:
                pe_bcast(rrow(R_A, ncols), ncols, a_bc[:, 0:ncols], 5,
                         ["r_a"], ABC_T)
                pe_bcast(rrow(R_MA, ncols), ncols, ma_bc[:, 0:ncols], 6,
                         ["r_ma"], ["ma_bc"])
            hT = h_t(m)[:, 0:ncols]
            htok = artok(m * CH, CH)
            if compute_cs:
                for kk in range(KC):
                    top(lambda e, kk=kk, wdst=wdst: e.matmul(
                        ps[4][0:1, 0:128], ones_col, wdst[:, kk * 128:(kk + 1) * 128],
                        start=(kk == 0), stop=(kk == KC - 1)),
                        reads=["onesc", wAtok(buf, kk // 2)], writes=["ps4"])
                aop(lambda e, m=m: e.copy(rrow(CSW1_O + m * 128, 128),
                                          ps[4][0:1, 0:128]),
                    reads=["ps4"], writes=["cs_w1_%d" % m])
                top(lambda e, m=m, pb=pb: e.matmul(
                    pb, rrow(CSW1_O + m * 128, 128), rrow(R_NEGM, ncols),
                    start=False, stop=True),
                    reads=["cs_w1_%d" % m, "r_negm"], writes=[pt])
                vop(lambda e, hT=hT, pb=pb: e.tensor_mul(hT, pb, a_bc[:, 0:ncols]),
                    reads=[pt] + ABC_T, writes=htok)
            else:
                vop(lambda e, hT=hT, pb=pb: e.tensor_mul(hT, pb, a_bc[:, 0:ncols]),
                    reads=[pt] + ABC_T, writes=htok)
                vop(lambda e, hT=hT, m=m: e.scalar_tensor_tensor(
                    hT, ma_bc[:, 0:ncols], csw1c[:, m:m + 1], hT, AL.mult, AL.add),
                    reads=htok + ["ma_bc", "csw1c"], writes=htok)
            sA = ar(O_SCR + 1024 + (m % 2) * CH, ncols)
            sAt = artok(O_SCR + 1024 + (m % 2) * CH, ncols)
            sB = ar(O_SCR + (m % 2) * CH, ncols)
            sBt = artok(O_SCR + (m % 2) * CH, ncols)
            aop(lambda e, hT=hT, sA=sA: e.activation(sA, hT, AF.Sin, scale=0.1),
                reads=htok, writes=sAt)
            aop(lambda e, hT=hT, sB=sB: e.activation(sB, hT, AF.Sin, bias=hpi_col,
                                                     scale=0.1),
                reads=htok + ["hpi"], writes=sBt)
            vop(lambda e, hT=hT, sA=sA, m=m: e.scalar_tensor_tensor(
                hT, sA, cs1[:, m:m + 1], hT, AL.mult, AL.add),
                reads=sAt + htok + ["cs1"], writes=htok)
            vop(lambda e, hT=hT, sB=sB, m=m: e.scalar_tensor_tensor(
                hT, sB, cc1[:, m:m + 1], hT, AL.mult, AL.add),
                reads=sBt + htok + ["cc1"], writes=htok)
            vop(lambda e, hT=hT, m=m: e.tensor_scalar(
                hT, hT, b1s[:, m:m + 1], 0.0, AL.add, AL.max),
                reads=htok + ["b1s"], writes=htok)
        if after_w1 is not None:
            after_w1()
        for ki in range(MT1):
            buf = ki % 2
            wdst = wbufB[:, buf * C:(buf + 1) * C]
            dma("sync", wdst, w2[ki * 128:(ki + 1) * 128, :],
                writes=wbt(buf * C, 1024))
            for mo in range(KC):
                top(lambda e, ki=ki, mo=mo, wdst=wdst: e.matmul(
                    ps[mo][:, 0:ncols], wdst[:, mo * 128:(mo + 1) * 128],
                    h_t(ki)[:, 0:ncols], start=(ki == 0), stop=(ki == MT1 - 1)),
                    reads=wbt(buf * C + mo * 128, 128) + artok(ki * CH, CH),
                    writes=[PT[mo]])
        for mo in range(KC):
            post_evac(mo, ps[mo][:, 0:ncols], PT[mo])

    ln_stats(xh_of, lambda kk: [XT(kk // 4)], 128)

    top(lambda e: e.matmul(ps[0][0:16, 0:32], wpos_s, relpe_s, start=True, stop=True),
        reads=["wpos", "relpe"], writes=["ps0"])
    aop(lambda e: e.copy(e_rep[0:16, :], ps[0][0:16, 0:32]),
        reads=["ps0"], writes=["e_rep"])
    for kk in range(1, 8):
        dma("gpsimd", e_rep[16 * kk:16 * (kk + 1), :], e_rep[0:16, :],
            reads=["e_rep"], writes=["e_rep_c%d" % kk])
    vop(lambda e: e.tensor_scalar_add(e_rep, e_rep, bstrainr),
        reads=["e_rep", "bstrainr"] + ["e_rep_c%d" % k for k in range(1, 8)],
        writes=["e_rep"])

    def _range_reduce(buf, tok):
        for _ in range(7):
            vop(lambda e, buf=buf: e.tensor_single_scalar(msk32, buf, PI, AL.is_gt),
                reads=[tok], writes=["msk32"])
            vop(lambda e, buf=buf: e.scalar_tensor_tensor(
                buf, msk32, -TWOPI, buf, AL.mult, AL.add),
                reads=["msk32", tok], writes=[tok])
            vop(lambda e, buf=buf: e.tensor_single_scalar(msk32, buf, -PI, AL.is_lt),
                reads=[tok], writes=["msk32"])
            vop(lambda e, buf=buf: e.scalar_tensor_tensor(
                buf, msk32, TWOPI, buf, AL.mult, AL.add),
                reads=["msk32", tok], writes=[tok])

    vop(lambda e: e.tensor_copy(tmp32b, tmp32), reads=["tmp32"], writes=["tmp32b"])
    _range_reduce(tmp32b, "tmp32b")
    aop(lambda e: e.activation(sc32, tmp32b, AF.Sin), reads=["tmp32b"],
        writes=["sc32"])
    vop(lambda e: e.tensor_mul(tmp32b, sc32, cs1), reads=["sc32", "cs1"],
        writes=["tmp32b"])
    vop(lambda e: e.tensor_scalar_add(sc32, tmp32, HPI), reads=["tmp32"],
        writes=["sc32"])
    _range_reduce(sc32, "sc32")
    aop(lambda e: e.activation(sc32, sc32, AF.Sin), reads=["sc32"], writes=["sc32"])
    vop(lambda e: e.tensor_mul(cc1, cc1, sc32), reads=["cc1", "sc32"], writes=["cc1"])
    vop(lambda e: e.tensor_add(cc1, cc1, tmp32b), reads=["cc1", "tmp32b"],
        writes=["cc1"])
    vop(lambda e: e.tensor_mul(cc1, cc1, cs1), reads=["cc1", "cs1"], writes=["cc1"])
    vop(lambda e: e.tensor_scalar_mul(cc1, cc1, 0.1), reads=["cc1"], writes=["cc1"])
    vop(lambda e: e.scalar_tensor_tensor(tmp32, b1s, 0.1, tmp32, AL.mult, AL.add),
        reads=["b1s", "tmp32"], writes=["tmp32"])
    vop(lambda e: e.tensor_copy(tmp32b, tmp32), reads=["tmp32"], writes=["tmp32b"])
    _range_reduce(tmp32b, "tmp32b")
    aop(lambda e: e.activation(tmp32b, tmp32b, AF.Sin), reads=["tmp32b"],
        writes=["tmp32b"])
    vop(lambda e: e.tensor_scalar_add(sc32, tmp32, HPI), reads=["tmp32"],
        writes=["sc32"])
    _range_reduce(sc32, "sc32")
    aop(lambda e: e.activation(sc32, sc32, AF.Sin), reads=["sc32"], writes=["sc32"])
    vop(lambda e: e.tensor_mul(cs1, cc1, sc32), reads=["cc1", "sc32"], writes=["cs1"])
    vop(lambda e: e.tensor_mul(cc1, cc1, tmp32b), reads=["cc1", "tmp32b"],
        writes=["cc1"])


    def boot_evac(mo, pb, pt):
        vop(lambda e, mo=mo, pb=pb: e.tensor_scalar(
            x1h_of(mo), pb, b2col[:, mo:mo + 1], 0.5, AL.add, AL.mult),
            reads=[pt, "b2col"], writes=[X1T(mo // 4)])
        vop(lambda e, mo=mo: e.tensor_add(x1h_of(mo), x1h_of(mo), xh_of(mo)),
            reads=[X1T(mo // 4), XT(mo // 4)], writes=[X1T(mo // 4)])

    ffn_pass(xh_of, lambda kk: [XT(kk // 4)], 128, True, boot_evac,
             skip_w1=2, skip_stats=True,
             after_w1=lambda: (load_w1(0), load_w1(1)))

    ln_stats(x1h_of, lambda kk: [X1T(kk // 4)], 128)
    pe_bcast(rrow(R_A, 128), 128, a_bc[:, 0:128], 5, ["r_a"], ABC_T)
    pe_bcast(rrow(R_MA, 128), 128, ma_bc[:, 0:128], 6, ["r_ma"], ["ma_bc"])
    spread_row(rrow(R_A, 128), 128, a2col[:, 4:5], ["r_a"], ["a2c4"])
    spread_row(rrow(CSW1_O, DFF), DFF, csw1c, ["cs_w1_%d" % m for m in range(MT1)],
               ["csw1c"])

    def load_wval(kk):
        buf = kk % 2
        wdst = wbufB[:, buf * C:(buf + 1) * C]
        dma("sync", wdst, wval[kk * 128:(kk + 1) * 128, :],
            writes=wbt(buf * C, 1024))

    for kk in range(KC):
        buf = kk % 2
        wdst = wbufB[:, buf * C:(buf + 1) * C]
        load_wval(kk)
        for n in range(2):
            top(lambda e, kk=kk, n=n, wdst=wdst: e.matmul(
                ps[n][:, 0:512], x1h_of(kk), wdst[:, n * 512:(n + 1) * 512],
                start=(kk == 0), stop=False),
                reads=wbt(buf * C + n * 512, 512) + [X1T(kk // 4)], writes=[PT[n]])
            top(lambda e, kk=kk, n=n, wdst=wdst: e.matmul(
                ps[4 + n][0:1, 0:512], ones_col, wdst[:, n * 512:(n + 1) * 512],
                start=(kk == 0), stop=(kk == KC - 1)),
                reads=["onesc"] + wbt(buf * C + n * 512, 512), writes=[PT[4 + n]])
    for n in range(2):
        aop(lambda e, n=n: e.copy(rrow(CSVAL_O + n * 512, 512),
                                  ps[4 + n][0:1, 0:512]),
            reads=[PT[4 + n]], writes=["cs_val%d" % n])
    for n in range(2):
        top(lambda e, n=n: e.matmul(
            ps[n][:, 0:512], rrow(R_NEGM, 128), rrow(CSVAL_O + n * 512, 512),
            start=False, stop=True),
            reads=["r_negm", "cs_val%d" % n], writes=[PT[n]])
        vop(lambda e, n=n: e.scalar_tensor_tensor(
            val_b[:, n * 512:(n + 1) * 512], ps[n][:, 0:512], a2col[:, 4:5],
            bvalr[:, n * 512:(n + 1) * 512], AL.mult, AL.add),
            reads=[PT[n], "a2c4", "bvalr"], writes=["val0"])

    def load_wdisp():
        dma("sync", wbufB, wdispT.ap(), writes=wbt(0, 2048))

    def wdisp_sl(kk, mi):
        return wbufB[:, kk * 256 + mi * 128: kk * 256 + (mi + 1) * 128]

    load_wdisp()
    for kk in range(KC):
        top(lambda e, kk=kk: e.matmul(ps[2][0:1, 0:256], ones_col,
                                      wbufB[:, kk * 256:(kk + 1) * 256],
                                      start=(kk == 0), stop=(kk == KC - 1)),
            reads=["onesc"] + wbt(kk * 256, 256), writes=["ps2"])
    aop(lambda e: e.copy(rrow(CSDISP_O, 256), ps[2][0:1, 0:256]),
        reads=["ps2"], writes=["cs_disp"])
    spread_row(rrow(CSDISP_O, 256), 256, csdispc, ["cs_disp"], ["csdispc"])

    for mi in range(2):
        pb = ps[2 + mi][:, 0:HALO]
        pt = PT[2 + mi]
        for kk in range(KC):
            top(lambda e, kk=kk, mi=mi, pb=pb: e.matmul(
                pb, wdisp_sl(kk, mi), x1h_of(kk)[:, 66:128],
                start=(kk == 0), stop=(kk == KC - 1)),
                reads=wbt(kk * 256 + mi * 128, 128) + [X1T(kk // 4)], writes=[pt])
        vop(lambda e, mi=mi, pb=pb: e.tensor_mul(
            disp_b[:, mi * TCOL: mi * TCOL + HALO], pb, a_bc[:, 66:128]),
            reads=[pt] + ABC_T, writes=DISP_T)
        vop(lambda e, mi=mi: e.scalar_tensor_tensor(
            disp_b[:, mi * TCOL: mi * TCOL + HALO], ma_bc[:, 66:128],
            csdispc[:, mi:mi + 1], disp_b[:, mi * TCOL: mi * TCOL + HALO],
            AL.mult, AL.add),
            reads=DISP_T + ["ma_bc", "csdispc"], writes=DISP_T)
        vop(lambda e, mi=mi: e.tensor_scalar_add(
            disp_b[:, mi * TCOL: mi * TCOL + HALO],
            disp_b[:, mi * TCOL: mi * TCOL + HALO], bdisps[:, mi:mi + 1]),
            reads=DISP_T + ["bdisps"], writes=DISP_T)

    BD3 = {"bd_state": 0, "bd_strain": 512, "bd_damage": 1024}
    BD2 = {"bd_si": 0, "bd_sj": 512}

    def load_bd3():
        for nm, off in BD3.items():
            dma("sync", wbufB[:, off:off + 512], din[nm].ap(),
                writes=wbt(off, 512))

    def load_bd2():
        for nm, off in BD2.items():
            dma("sync", wbufB[:, off:off + 512], din[nm].ap(),
                writes=wbt(off, 512))

    def bd_sl(base, kk, mi):
        return wbufB[:, base + kk * 256 + mi * 128: base + kk * 256 + (mi + 1) * 128]

    load_bd3()
    for (buf, btok, nm) in ((q_b, "q", "bd_state"), (p_b, "p", "bd_strain"),
                            (r_b, "r", "bd_damage")):
        for mi in range(2):
            pb = ps[4 + mi][:, 0:HALO]
            pt = PT[4 + mi]
            for kk in range(2):
                top(lambda e, kk=kk, mi=mi, pb=pb, nm=nm: e.matmul(
                    pb, bd_sl(BD3[nm], kk, mi),
                    disp_b[:, kk * TCOL: kk * TCOL + HALO],
                    start=(kk == 0), stop=(kk == 1)),
                    reads=wbt(BD3[nm] + kk * 256 + mi * 128, 128) + DISP_T,
                    writes=[pt])
            aop(lambda e, buf=buf, mi=mi, pb=pb: e.copy(
                buf[:, mi * TCOL: mi * TCOL + HALO], pb),
                reads=[pt], writes=[btok])
    vop(lambda e: e.tensor_mul(
        q_b.rearrange("p (h t) -> p h t", h=2)[:, :, 0:HALO],
        q_b.rearrange("p (h t) -> p h t", h=2)[:, :, 0:HALO],
        qmask_s[:, None, :].broadcast_to([128, 2, HALO])),
        reads=["q", "qmask"], writes=["q"])

    def x_k(kk):
        return x_b[:, kk * CH:(kk + 1) * CH]

    def x1_k(kk):
        return x1_b[:, kk * CH:(kk + 1) * CH]

    def sh3(buf, off, n, shift=0):
        return buf.rearrange("p (h t) -> p h t", h=2)[
            :, :, off + shift: off + shift + n]

    def sh3a(off0, off, n, shift=0):
        return arena[:, off0:off0 + 2 * TCOL].rearrange(
            "p (h t) -> p h t", h=2)[:, :, off + shift: off + shift + n]

    def emit_sigs(c):
        order = (0, 2, 1, 3)
        for i in order:
            sig = ar(O_SIG + i * CH, CH)
            sigt = artok(O_SIG + i * CH, CH)
            aop(lambda e, i=i, sig=sig: e.activation(
                sig, ps[4 + i][:, 0:CH], AF.Sigmoid, bias=bdmgor),
                reads=[PT[4 + i], "bdmgor"], writes=sigt)
        for i in order:
            sig = ar(O_SIG + i * CH, CH)
            sigt = artok(O_SIG + i * CH, CH)
            est = ar(O_ESTK + i * CH, CH)
            estt = artok(O_ESTK + i * CH, CH)
            vop(lambda e, i=i, sig=sig, est=est: e.scalar_tensor_tensor(
                est, sig, -10.0, ps[i][:, 0:CH], AL.mult, AL.add),
                reads=sigt + [PT[i]], writes=estt)
        for i in order:
            est = ar(O_ESTK + i * CH, CH)
            estt = artok(O_ESTK + i * CH, CH)
            aop(lambda e, est=est: e.activation(est, est, AF.Exp),
                reads=estt, writes=estt)
        if c == 0:
            for i in order:
                est = ar(O_ESTK + i * CH, CH)
                estt = artok(O_ESTK + i * CH, CH)
                ab = i % 2
                vop(lambda e, est=est, ab=ab: e.tensor_mul(
                    est[:, 0:31], est[:, 0:31],
                    expmask_s[:, ab * 31:(ab + 1) * 31]),
                    reads=estt + ["expmask"], writes=estt)

    for c in range(NCHUNK):
        if c > 0:
            for (buf, btok) in ((q_b, "q"), (p_b, "p"), (r_b, "r")):
                vop(lambda e, buf=buf: e.tensor_copy(
                    sh3(buf, 0, HALO), sh3(buf, CH, HALO)),
                    reads=[btok], writes=[btok])
            vop(lambda e: e.tensor_copy(val_b[:, 0:C], val_b[:, 4 * C:5 * C]),
                reads=["val4"], writes=["val0"])

        for kk in range(KC):
            dma("sync", x_b[:, kk * CH:(kk + 1) * CH],
                xT[kk * 128:(kk + 1) * 128, 128 + c * CH:128 + (c + 1) * CH],
                writes=[XT(kk)])
        def ffn1_evac(mo, pb, pt):
            vop(lambda e, mo=mo, pb=pb: e.tensor_scalar(
                x1_k(mo), pb, b2col[:, mo:mo + 1], 0.5, AL.add, AL.mult),
                reads=[pt, "b2col"], writes=[X1T(mo)])
            vop(lambda e, mo=mo: e.tensor_add(x1_k(mo), x1_k(mo), x_k(mo)),
                reads=[X1T(mo), XT(mo)], writes=[X1T(mo)])

        ffn_pass(x_k, lambda kk: [XT(kk)], CH, False, ffn1_evac, skip_w1=2)

        dma("gpsimd", invc_b, din["invc"][c], writes=INVC_T, sem_key="d:invc")
        ln_stats(x1_k, lambda kk: [X1T(kk)], CH)

        load_wdisp()
        for mi in range(2):
            pb = ps[mi][:, 0:CH]
            pt = PT[mi]
            for kk in range(KC):
                top(lambda e, kk=kk, mi=mi, pb=pb: e.matmul(
                    pb, wdisp_sl(kk, mi), x1_k(kk),
                    start=(kk == 0), stop=(kk == KC - 1)),
                    reads=wbt(kk * 256 + mi * 128, 128) + [X1T(kk)], writes=[pt])
            if mi == 0:
                pe_bcast(rrow(R_A, CH), CH, a_bc, 5, ["r_a"], ABC_T)
                pe_bcast(rrow(R_MA, CH), CH, ma_bc, 6, ["r_ma"], ["ma_bc"])
                for j in range(NJT):
                    spread_row(rrow(R_A + j * 128, 128), 128, a2col[:, j:j + 1],
                               ["r_a"], ["a2c%d" % j])
            vop(lambda e, mi=mi, pb=pb: e.tensor_mul(
                disp_b[:, mi * TCOL + HALO:(mi + 1) * TCOL], pb, a_bc),
                reads=[pt] + ABC_T, writes=DISP_T)
            vop(lambda e, mi=mi: e.scalar_tensor_tensor(
                disp_b[:, mi * TCOL + HALO:(mi + 1) * TCOL], ma_bc,
                csdispc[:, mi:mi + 1],
                disp_b[:, mi * TCOL + HALO:(mi + 1) * TCOL], AL.mult, AL.add),
                reads=DISP_T + ["ma_bc", "csdispc"], writes=DISP_T)
            vop(lambda e, mi=mi: e.tensor_scalar_add(
                disp_b[:, mi * TCOL + HALO:(mi + 1) * TCOL],
                disp_b[:, mi * TCOL + HALO:(mi + 1) * TCOL], bdisps[:, mi:mi + 1]),
                reads=DISP_T + ["bdisps"], writes=DISP_T)

        load_bd3()
        for (buf, btok, nm) in ((q_b, "q", "bd_state"), (p_b, "p", "bd_strain"),
                                (r_b, "r", "bd_damage")):
            for mi in range(2):
                pb = ps[2 + mi][:, 0:CH]
                pt = PT[2 + mi]
                for kk in range(2):
                    top(lambda e, kk=kk, mi=mi, pb=pb, nm=nm: e.matmul(
                        pb, bd_sl(BD3[nm], kk, mi),
                        disp_b[:, kk * TCOL + HALO:(kk + 1) * TCOL],
                        start=(kk == 0), stop=(kk == 1)),
                        reads=wbt(BD3[nm] + kk * 256 + mi * 128, 128) + DISP_T,
                        writes=[pt])
                aop(lambda e, buf=buf, mi=mi, pb=pb: e.copy(
                    buf[:, mi * TCOL + HALO:(mi + 1) * TCOL], pb),
                    reads=[pt], writes=[btok])

        seq = [(q_b, ["q"]), (sq_b, SQ_T), (st_b, ["st"]),
               (sq_b, SQ_T), (st_b, ["st"]), (sq_b, SQ_T)]
        for i, sft in enumerate((1, 2, 4, 8, 16)):
            off = 2 * sft - 1
            nn = TCOL - off
            sbuf, srd = seq[i]
            dbuf, dwr = seq[i + 1]
            vop(lambda e, sbuf=sbuf, dbuf=dbuf, sft=sft, nn=nn, off=off:
                e.tensor_add(sh3(dbuf, off, nn), sh3(sbuf, off, nn),
                             sh3(sbuf, off, nn, -sft)),
                reads=srd, writes=dwr)
        nn = TCOL - 31
        vop(lambda e, nn=nn: e.tensor_mul(
            sh3(sq_b, 31, nn), sh3(sq_b, 31, nn),
            invc_b[:, None, :].broadcast_to([128, 2, nn])),
            reads=SQ_T + INVC_T, writes=SQ_T)
        vop(lambda e, nn=nn: e.tensor_sub(
            sh3(st_b, 31, nn), sh3(sq_b, 31, nn), sh3(q_b, 31, nn)),
            reads=SQ_T + ["q"], writes=["st"])
        vop(lambda e, nn=nn: e.tensor_scalar_add(
            sh3(st_b, 31, nn), sh3(st_b, 31, nn), bstater),
            reads=["st", "bstater"], writes=["st"])

        for kk in range(KC):
            buf = kk % 2
            wdst = wbufB[:, buf * C:(buf + 1) * C]
            load_wval(kk)
            for j in range(NJT):
                for n in range(2):
                    top(lambda e, kk=kk, n=n, j=j, wdst=wdst: e.matmul(
                        ps[2 * j + n][:, 0:512],
                        x1_b[:, kk * CH + j * 128: kk * CH + (j + 1) * 128],
                        wdst[:, n * 512:(n + 1) * 512],
                        start=(kk == 0), stop=False),
                        reads=wbt(buf * C + n * 512, 512) + [X1T(kk)],
                        writes=[PT[2 * j + n]])
        for j in range(NJT):
            for n in range(2):
                top(lambda e, n=n, j=j: e.matmul(
                    ps[2 * j + n][:, 0:512], rrow(R_NEGM + j * 128, 128),
                    rrow(CSVAL_O + n * 512, 512), start=False, stop=True),
                    reads=["r_negm", "cs_val%d" % n], writes=[PT[2 * j + n]])
                vop(lambda e, n=n, j=j: e.scalar_tensor_tensor(
                    val_b[:, (1 + j) * C + n * 512:(1 + j) * C + (n + 1) * 512],
                    ps[2 * j + n][:, 0:512], a2col[:, j:j + 1],
                    bvalr[:, n * 512:(n + 1) * 512], AL.mult, AL.add),
                    reads=[PT[2 * j + n], "a2c%d" % j, "bvalr"],
                    writes=["val%d" % (1 + j)])

        load_bd2()
        for mi in range(2):
            pb = ps[4 + mi][:, 0:CH]
            pt = PT[4 + mi]
            for kk in range(2):
                top(lambda e, kk=kk, mi=mi, pb=pb: e.matmul(
                    pb, bd_sl(BD2["bd_si"], kk, mi),
                    st_b[:, kk * TCOL + HALO:(kk + 1) * TCOL],
                    start=(kk == 0), stop=(kk == 1)),
                    reads=wbt(BD2["bd_si"] + kk * 256 + mi * 128, 128) + ["st"],
                    writes=[pt])
            vop(lambda e, mi=mi, pb=pb: e.tensor_sub(
                bv_b[:, mi * TCOL + HALO:(mi + 1) * TCOL], pb,
                p_b[:, mi * TCOL + HALO:(mi + 1) * TCOL]),
                reads=[pt, "p"], writes=["bv"])
        for (c0, cn, pbi) in ((HALO, CH, 6), (31, 31, 7)):
            for mi in range(2):
                pb = ps[pbi][:, 0:cn]
                pt = PT[pbi]
                for kk in range(2):
                    top(lambda e, kk=kk, mi=mi, pb=pb, c0=c0, cn=cn: e.matmul(
                        pb, bd_sl(BD2["bd_sj"], kk, mi),
                        st_b[:, kk * TCOL + c0: kk * TCOL + c0 + cn],
                        start=(kk == 0), stop=(kk == 1)),
                        reads=wbt(BD2["bd_sj"] + kk * 256 + mi * 128, 128)
                        + ["st"], writes=[pt])
                vop(lambda e, mi=mi, pb=pb, c0=c0, cn=cn: e.tensor_add(
                    p_b[:, mi * TCOL + c0: mi * TCOL + c0 + cn],
                    p_b[:, mi * TCOL + c0: mi * TCOL + c0 + cn], pb),
                    reads=[pt, "p"], writes=["p"])

        load_wcp(0)
        load_wcp(1)

        for d in range(DELTA):
            dl, hi = d % 16, d // 16
            rot = d % 2
            pre_b = ar(O_PRE_B + rot * 1024, 1024).rearrange("p (h n) -> p h n", h=2)
            pre_bt = artok(O_PRE_B + rot * 1024, 1024)
            act_b = ar(O_ACT_B + rot * 1024, 1024).rearrange("p (h n) -> p h n", h=2)
            act_bt = artok(O_ACT_B + rot * 1024, 1024)
            pre_d = ar(O_PRE_D + rot * 1024, 1024).rearrange("p (h n) -> p h n", h=2)
            pre_dt = artok(O_PRE_D + rot * 1024, 1024)
            act_d = ar(O_ACT_D + rot * 1024, 1024).rearrange("p (h n) -> p h n", h=2)
            act_dt = artok(O_ACT_D + rot * 1024, 1024)
            vop(lambda e, d=d, pre_b=pre_b: e.tensor_add(
                pre_b, sh3(p_b, 31 + d, 512), sh3(bv_b, HALO, 512)),
                reads=["p", "bv"], writes=pre_bt)
            vop(lambda e, d=d, pre_d=pre_d: e.tensor_sub(
                pre_d, sh3(r_b, 31 + d, 512), sh3(r_b, HALO, 512)),
                reads=["r"], writes=pre_dt)
            aop(lambda e, d=d, pre_b=pre_b, act_b=act_b: e.activation(
                act_b, pre_b, AF.Gelu, bias=e_rep[:, d:d + 1]),
                reads=pre_bt + ["e_rep"], writes=act_bt)
            aop(lambda e, d=d, pre_d=pre_d, act_d=act_d: e.activation(
                act_d, pre_d, AF.Gelu, bias=bdmgr),
                reads=pre_dt + ["bdmgr"], writes=act_dt)
            for half in range(2):
                st_i = half * 2 + hi
                top(lambda e, dl=dl, half=half, st_i=st_i, act_b=act_b: e.matmul(
                    ps[st_i][:, 0:CH], mast_b[:, 15 - dl:143 - dl],
                    act_b[:, half, :], start=(dl == 0), stop=(dl == 15)),
                    reads=["mastb"] + act_bt, writes=[PT[st_i]])
                top(lambda e, dl=dl, half=half, st_i=st_i, act_d=act_d: e.matmul(
                    ps[4 + st_i][:, 0:CH], mast_d[:, 15 - dl:143 - dl],
                    act_d[:, half, :], start=(dl == 0), stop=(dl == 15)),
                    reads=["mastd"] + act_dt, writes=[PT[4 + st_i]])
        emit_sigs(c)

        for wi in range(2):
            vop(lambda e, wi=wi: e.memset(wtm2[wi], 0.0), writes=WTM_T2[wi])

        def shear_read(j, h, engine):
            reg = j % 2
            slot = (j % 2) * 5 + h % 5
            if slot < 2:
                wshv = ar(O_WSH + slot * PP, PP)
            elif slot < 6:
                wshv = ar(O_WSHX + (slot - 2) * PP, PP)
            else:
                wshv = ar(O_SQ + (slot - 6) * PP, PP)
            wsht = ["wsh%d" % slot]
            dma(engine, wshv,
                AP(scratch, reg * SREG + h * PP, [[SROW - 1, 128], [1, PP]]),
                reads=["scr%d" % reg], writes=wsht)
            return wshv, wsht

        def stage1(j):
            for i in (0, 2, 1, 3):
                top(lambda e, i=i, j=j: e.transpose(
                    ps[i][:, 0:128], ar(O_ESTK + i * CH + j * 128, 128), iden_s),
                    reads=artok(O_ESTK + i * CH + j * 128, 128) + ["iden"],
                    writes=[PT[i]])
            for i in range(4):
                half, ab = i // 2, i % 2
                vop(lambda e, i=i, half=half, ab=ab: e.reduce_sum(
                    sums[:, 16 * ab + half * 8:16 * ab + half * 8 + 8],
                    ps[i][:, 0:128].rearrange("p (h d) -> p h d", h=8), axis=AX.X),
                    reads=[PT[i]], writes=["sums"])
            vop(lambda e: e.tensor_add(sums[:, 32:48], sums[:, 0:16],
                                       sums[:, 16:32]),
                reads=["sums"], writes=["sums"])
            vop(lambda e: e.reciprocal(sums[:, 32:48], sums[:, 32:48]),
                reads=["sums"], writes=["sums"])
            wi = j % 2
            for i in range(4):
                half, ab = i // 2, i % 2
                dst = AP(sb["arena"], (O_WTM if wi == 0 else O_WTM1)
                         + half * 8 * PP + ab * 16,
                         [[ARENA, 128], [PP, 8], [1, 16]])
                rsl = sums[:, 32 + half * 8:32 + half * 8 + 8][:, :, None] \
                    .broadcast_to([128, 8, 16])
                vop(lambda e, i=i, dst=dst, rsl=rsl: e.tensor_mul(
                    dst, ps[i][:, 0:128].rearrange("p (h d) -> p h d", h=8), rsl),
                    reads=[PT[i], "sums"], writes=WTM_T2[wi])
            reg = j % 2
            dma("gpsimd", AP(scratch, reg * SREG + 97, [[SROW, 128], [1, SROW]]),
                wtm2[wi], reads=WTM_T2[wi] + ["sczero%d" % reg],
                writes=["scr%d" % reg])
            return {h: shear_read(j, h, "gpsimd") for h in range(5)}

        def banded(j, h):
            wblP = ar(O_WBL + (h % 2) * 256, 128)
            wblC = ar(O_WBL + (h % 2) * 256 + 128, 128)
            wblt = ["wbl%d" % (h % 2)]
            pair = h // 2
            obank = ps[6 + pair % 2]
            obt = PT[6 + pair % 2]
            orng = (h % 2) * 64
            top(lambda e, j=j, h=h, wblP=wblP, obank=obank, orng=orng: e.matmul(
                obank[orng:orng + 64, 0:128],
                val_b[:, j * C + h * 64:j * C + (h + 1) * 64], wblP,
                start=True, stop=False),
                reads=wblt + ["val%d" % j], writes=[obt])
            top(lambda e, j=j, h=h, wblC=wblC, obank=obank, orng=orng: e.matmul(
                obank[orng:orng + 64, 0:128],
                val_b[:, (1 + j) * C + h * 64:(1 + j) * C + (h + 1) * 64], wblC,
                start=False, stop=True),
                reads=wblt + ["val%d" % (1 + j)], writes=[obt])
            if h % 2 == 1:
                aop(lambda e, pair=pair, j=j, obank=obank: e.copy(
                    ar(O_ATT + pair * CH + j * 128, 128), obank[:, 0:128]),
                    reads=[obt], writes=artok(O_ATT + pair * CH + j * 128, 128))

        def stage2(j, rds):
            for h in range(NH):
                wshv, wsht = rds[h]
                pbank = ps[4 + (h % 2)]
                pbt = PT[4 + (h % 2)]
                top(lambda e, wshv=wshv, pbank=pbank: e.transpose(
                    pbank[:, 0:128], wshv[:, 0:128], iden_s),
                    reads=wsht + ["iden"], writes=[pbt])
                top(lambda e, wshv=wshv, pbank=pbank: e.transpose(
                    pbank[:, 128:256], wshv[:, 128:256], iden_s),
                    reads=wsht + ["iden"], writes=[pbt])
                if h + 5 < NH:
                    rds[h + 5] = shear_read(j, h + 5, "sync")
                wblP = ar(O_WBL + (h % 2) * 256, 128)
                wblC = ar(O_WBL + (h % 2) * 256 + 128, 128)
                wblt = ["wbl%d" % (h % 2)]
                aop(lambda e, wblP=wblP, pbank=pbank: e.copy(wblP, pbank[:, 0:128]),
                    reads=[pbt], writes=wblt)
                aop(lambda e, wblC=wblC, pbank=pbank: e.copy(wblC, pbank[:, 128:256]),
                    reads=[pbt], writes=wblt)
                if h > 0:
                    banded(j, h - 1)
            banded(j, NH - 1)

        pend = {}
        for j in range(NJT):
            pend[j] = stage1(j)
            if j > 0:
                stage2(j - 1, pend.pop(j - 1))
        stage2(NJT - 1, pend.pop(NJT - 1))

        for mo in range(KC):
            buf = mo % 2
            wdst = wbufA[:, buf * KC * 128:(buf + 1) * KC * 128]
            if mo >= 2:
                load_wcp(mo)
            pb = ps[mo % 2][:, 0:CH]
            pt = PT[mo % 2]
            for kk in range(KC):
                top(lambda e, kk=kk, pb=pb, wdst=wdst: e.matmul(
                    pb, wdst[:, kk * 128:(kk + 1) * 128], ar(O_ATT + kk * CH, CH),
                    start=(kk == 0), stop=(kk == KC - 1)),
                    reads=[wAtok(buf, kk // 2)] + artok(O_ATT + kk * CH, CH),
                    writes=[pt])
            vop(lambda e, mo=mo, pb=pb: e.tensor_scalar_add(
                x_k(mo), pb, bcpcol[:, mo:mo + 1]),
                reads=[pt, "bcpcol"], writes=[XT(mo)])
            vop(lambda e, mo=mo: e.tensor_add(x_k(mo), x_k(mo), x1_k(mo)),
                reads=[XT(mo), X1T(mo)], writes=[XT(mo)])
        load_w1(0)
        load_w1(1)

        def ffn2_evac(mo, pb, pt, c=c):
            slot = mo % 4
            sv = ar(O_STG4 + slot * CH, CH)
            stok = artok(O_STG4 + slot * CH, CH)
            vop(lambda e, mo=mo, pb=pb, sv=sv: e.tensor_scalar(
                sv, pb, b2col[:, mo:mo + 1], 0.5, AL.add, AL.mult),
                reads=[pt, "b2col"], writes=stok)
            vop(lambda e, mo=mo, sv=sv: e.tensor_add(sv, sv, x_k(mo)),
                reads=stok + [XT(mo)], writes=stok)
            dma("gpsimd" if slot % 2 else "sync",
                out_d[mo * 128:(mo + 1) * 128, c * CH:(c + 1) * CH],
                sv, reads=stok, writes=["outd%d_%d" % (c, mo)],
                always_inc=True,
                sem_key=("dg:out%d" if slot % 2 else "d:out%d") % slot)

        ffn_pass(x_k, lambda kk: [XT(kk)], CH, False, ffn2_evac, skip_w1=2,
                 after_w1=(lambda: (load_w1(0), load_w1(1)))
                 if c < NCHUNK - 1 else None)

    S.op("gpsimd", lambda e: e.engine_nop(),
         reads=["outd%d_%d" % (c, mo) for c in range(NCHUNK) for mo in range(KC)])

    S.finalize(nc, total_keys=("d:cst", "d:cst0"))
    with nc.Block() as block:
        @block.tensor
        def _(eng):
            S.emit("tensor", eng)

        @block.vector
        def _(eng):
            S.emit("vector", eng)

        @block.scalar
        def _(eng):
            S.emit("scalar", eng)

        @block.gpsimd
        def _(eng):
            S.emit("gpsimd", eng)

        @block.sync
        def _(eng):
            S.emit("sync", eng)

    _CACHE["nc"] = nc
    _CACHE["stats"] = S.stats
    _CACHE["layout"] = dict(O_ESTK=O_ESTK, O_ATT=O_ATT, O_SIG=O_SIG,
                            O_WSH=O_WSH, O_WBL=O_WBL, O_STG=O_STG,
                            O_DISP=O_DISP, O_SQ=O_SQ)
    return nc


def kernel(**inputs):
    ii = {k: np.asarray(v) for k, v in inputs.items()}
    x = ii["L0_attn_x"]

    def blockdiag_re(w):
        bd = np.zeros((256, 256), np.float32)
        for h in range(NH):
            bd[h * BD:(h + 1) * BD, h * BD:(h + 1) * BD] = w
        return np.concatenate([bd[0:128, :], bd[128:256, :]], axis=1)

    def master(wout):
        m = np.zeros((128, 143), np.float32)
        for h in range(8):
            for k in range(BD):
                m[h * 16 + k, h * 16 + 15] = wout[k, 0]
        return m

    wd = ii["L0_attn_w_disp"]
    wdispT = np.zeros((128, KC * 256), np.float32)
    for kk in range(KC):
        wdispT[:, kk * 256:(kk + 1) * 256] = wd[kk * 128:(kk + 1) * 128, :]

    common = {
        "w1re": np.ascontiguousarray(
            ii["L0_ffn_W1"].reshape(KC, 128, MT1, 128).transpose(2, 1, 0, 3)
            .reshape(MT1, 128, KC * 128)),
        "w2": ii["L0_ffn_W2"],
        "wval": ii["L0_attn_w_val"],
        "wcpre": np.ascontiguousarray(
            ii["L0_attn_w_cproj"].reshape(KC, 128, KC, 128).transpose(2, 1, 0, 3)
            .reshape(KC, 128, KC * 128)),
        "wdispT": wdispT,
        "bd_state": blockdiag_re(ii["L0_attn_w_state"]),
        "bd_strain": blockdiag_re(ii["L0_attn_w_strain"]),
        "bd_damage": blockdiag_re(ii["L0_attn_w_damage"]),
        "bd_si": blockdiag_re(ii["L0_attn_w_state_i"]),
        "bd_sj": blockdiag_re(ii["L0_attn_w_state_j"]),
        "master_bond": master(ii["L0_attn_w_bond_out"]),
        "master_dmg": master(ii["L0_attn_w_damage_out"]),
        "relpeT": np.ascontiguousarray(ii["L0_attn_rel_pos_emb"].T),
        "wpos": ii["L0_attn_w_pos"],
        "theta_t": np.ascontiguousarray(ii["L0_ffn_quantum_phases"].reshape(32, 128).T),
        "wreal_t": np.ascontiguousarray(
            ii["L0_ffn_quantum_weights_real"].reshape(32, 128).T),
        "wimag_t": np.ascontiguousarray(
            ii["L0_ffn_quantum_weights_imag"].reshape(32, 128).T),
        "b1_t": np.ascontiguousarray(ii["L0_ffn_b1"].reshape(32, 128).T),
        "b2_t": np.ascontiguousarray(ii["L0_ffn_b2"].reshape(8, 128).T),
        "bdisp_t": np.ascontiguousarray(ii["L0_attn_b_disp"].reshape(2, 128).T),
        "bval_rep": np.broadcast_to(ii["L0_attn_b_val"], (128, C)),
        "bcproj_t": np.ascontiguousarray(ii["L0_attn_b_cproj"].reshape(8, 128).T),
        "bstate_rep": np.tile(ii["L0_attn_b_state"], 8).reshape(128, 1),
        "bstrain_rep": np.tile(ii["L0_attn_b_strain"], 8).reshape(128, 1),
        "bdmg_rep": np.tile(ii["L0_attn_b_damage"], 8).reshape(128, 1),
        "bdmgout_rep": np.broadcast_to(
            ii["L0_attn_b_damage_out"].reshape(1, 1), (128, 1)),
        "iden": np.eye(128, dtype=np.float32),
        "ones_col": np.ones((128, 1), np.float32),
        "hpi_col": np.full((128, 1), np.pi / 2, np.float32),
    }
    common = {k: np.ascontiguousarray(v, dtype=np.float32) for k, v in common.items()}

    in_maps = []
    for core in range(NCORE):
        b, kchunk = core // 4, core % 4
        start = kchunk * TC
        if kchunk == 0:
            xs = np.concatenate(
                [np.zeros((128, C), np.float32), x[b, 0:TC, :]], axis=0)
        else:
            xs = x[b, start - 128:start + TC, :]
        m = dict(common)
        m["xT"] = np.ascontiguousarray(xs.T, dtype=np.float32)
        qm = np.ones((128, HALO), np.float32)
        em = np.ones((128, 62), np.float32)
        inv = np.full((NCHUNK, 128, TCOL - 31), 1.0 / DELTA, np.float32)
        if kchunk == 0:
            qm[:] = 0.0
            dl = np.arange(128) % 16
            tt = np.arange(31)
            for ab in range(2):
                dd = ab * 16 + dl
                em[:, ab * 31:(ab + 1) * 31] = (
                    dd[:, None] >= 31 - tt[None, :]).astype(np.float32)
            tok = np.arange(-31, CH)
            cnt = np.clip(tok + 1, 1, DELTA).astype(np.float32)
            cnt[tok < 0] = DELTA
            inv[0, :, :] = 1.0 / cnt[None, :]
        m["qmask"] = qm
        m["expmask"] = em
        m["invc"] = np.ascontiguousarray(inv)
        in_maps.append(m)

    nc = build_nc()
    res = run_bass_kernel_spmd(nc, in_maps, core_ids=list(range(NCORE)))
    out = np.empty((B, T, C), np.float32)
    for core in range(NCORE):
        b, kchunk = core // 4, core % 4
        out[b, kchunk * TC:(kchunk + 1) * TC, :] = res.results[core]["out"].T
    return out


# version 13
# speedup vs baseline: 1.0684x; 1.0684x over previous
import math
import numpy as np

import concourse.bass as bass
import concourse.mybir as mybir
from concourse.bass import AP
from concourse.bass_utils import run_bass_kernel_spmd

F32 = mybir.dt.float32
F32R = mybir.dt.float32r
AF = mybir.ActivationFunctionType
AL = mybir.AluOpType
AX = mybir.AxisListType

B, T, C = 2, 8192, 1024
NH, BD, HS, DELTA = 16, 16, 64, 32
DFF = 4096
NCORE = 8
TC = 2048
NCHUNK, CH = 4, 512
NJT, JT = 4, 128
HALO = 62
TCOL = HALO + CH
KC = C // 128
MT1 = DFF // 128
PP = 256
SROW = NH * PP
SREG = 97 + 128 * SROW
EPS = 1e-5
TWOPI, PI, HPI = 2.0 * math.pi, math.pi, math.pi / 2.0

_CACHE = {}


class _Op:
    __slots__ = ("engine", "fn", "deps", "needs_inc", "val", "sem_key")

    def __init__(self, engine, fn, always_inc, sem_key):
        self.engine = engine
        self.fn = fn
        self.deps = []
        self.needs_inc = always_inc
        self.val = None
        self.sem_key = sem_key


class Sched:

    DMA_ENGINES = ("sync", "gpsimd")

    def __init__(self):
        self.order = {e: [] for e in ("tensor", "vector", "scalar", "gpsimd", "sync")}
        self.seq = []
        self.last_w = {}
        self.readers = {}

    def op(self, engine, fn, reads=(), writes=(), always_inc=False, sem_key=None):
        if engine in self.DMA_ENGINES:
            if sem_key is None:
                pre = "dg:" if engine == "gpsimd" else "d:"
                sem_key = pre + (writes[0] if writes else "misc")
        else:
            sem_key = engine
        o = _Op(engine, fn, always_inc or engine in self.DMA_ENGINES, sem_key)
        deps = []
        for t in reads:
            p = self.last_w.get(t)
            if p is not None:
                deps.append(p)
        for t in writes:
            deps.extend(self.readers.get(t, ()))
            p = self.last_w.get(t)
            if p is not None:
                deps.append(p)
        for p in set(deps):
            if p is o:
                continue
            if p.engine == "tensor" and engine == "tensor":
                continue
            o.deps.append(p)
            p.needs_inc = True
        for t in reads:
            self.readers.setdefault(t, []).append(o)
        for t in writes:
            self.last_w[t] = o
            self.readers[t] = []
        self.order[engine].append(o)
        self.seq.append(o)
        return o

    def finalize(self, nc, total_keys=()):
        keys = []
        for e, lst in self.order.items():
            for o in lst:
                if o.sem_key not in keys:
                    keys.append(o.sem_key)
        sems = {k: nc.alloc_semaphore("s_" + k.replace(":", "_")) for k in keys}
        cnt = {k: 0 for k in keys}
        for o in self.seq:
            if o.needs_inc:
                cnt[o.sem_key] += 16 if o.engine in self.DMA_ENGINES else 1
                o.val = cnt[o.sem_key]
        for e, lst in self.order.items():
            for o in lst:
                if o.sem_key in total_keys and o.val is not None:
                    o.val = cnt[o.sem_key]
        self.emitted = {}
        nwaits = 0
        for e, lst in self.order.items():
            step = 16 if e in self.DMA_ENGINES else 1
            sat = {}
            out = []
            for o in lst:
                waits = {}
                for p in o.deps:
                    assert p.val is not None, "dep without inc"
                    if waits.get(p.sem_key, 0) < p.val:
                        waits[p.sem_key] = p.val
                wl = []
                for s, v in waits.items():
                    if sat.get(s, 0) >= v:
                        continue
                    wl.append((sems[s], v))
                    sat[s] = v
                nwaits += len(wl)
                out.append((tuple(wl), o.fn,
                            sems[o.sem_key] if o.needs_inc else None, step))
            self.emitted[e] = out
        self.stats = {e: len(l) for e, l in self.order.items()}
        self.stats["waits"] = nwaits
        self.stats["sems"] = len(keys)

    def emit(self, engine, eng):
        for waits, fn, sem, inck in self.emitted[engine]:
            for s, v in waits:
                eng.wait_ge(s, v)
            bi = fn(eng)
            if sem is not None and bi is not None:
                bi.then_inc(sem, inck)


def build_nc():
    if "nc" in _CACHE:
        return _CACHE["nc"]
    nc = bass.Bass("TRN2", target_bir_lowering=False)

    din = {}

    def DI(name, shape):
        din[name] = nc.dram_tensor(name, list(shape), F32, kind="ExternalInput")
        return din[name]

    xT = DI("xT", (C, 128 + TC))
    w1re = DI("w1re", (MT1, 128, KC * 128))
    w2 = DI("w2", (DFF, C))
    wval = DI("wval", (C, C))
    wcpre = DI("wcpre", (KC, 128, KC * 128))
    wdispT = DI("wdispT", (128, KC * 256))
    for nm in ("bd_state", "bd_strain", "bd_damage", "bd_si", "bd_sj"):
        DI(nm, (128, 2 * 256))
    DI("master_bond", (128, 143))
    DI("master_dmg", (128, 143))
    DI("relpeT", (BD, DELTA))
    DI("wpos", (BD, BD))
    DI("theta_t", (128, 32))
    DI("wreal_t", (128, 32))
    DI("wimag_t", (128, 32))
    DI("b1_t", (128, 32))
    DI("b2_t", (128, 8))
    DI("bdisp_t", (128, 2))
    DI("bval_rep", (128, C))
    DI("bcproj_t", (128, 8))
    DI("bstate_rep", (128, 1))
    DI("bstrain_rep", (128, 1))
    DI("bdmg_rep", (128, 1))
    DI("bdmgout_rep", (128, 1))
    DI("iden", (128, 128))
    DI("ones_col", (128, 1))
    DI("hpi_col", (128, 1))
    DI("invc", (NCHUNK, 128, TCOL - 31))
    DI("qmask", (128, HALO))
    DI("expmask", (128, 62))
    out_d = nc.dram_tensor("out", [C, TC], F32, kind="ExternalOutput")
    scratch = nc.dram_tensor("scratch", [2 * SREG], F32, kind="Internal")
    rowscr = nc.dram_tensor("rowscr", [8 * 1024], F32, kind="Internal")

    sb = {}

    def SA(name, cols, parts=128):
        sb[name] = nc.alloc_sbuf_tensor(name, [parts, cols], F32)
        return sb[name].ap()

    x_b = SA("x_b", KC * CH)
    x1_b = SA("x1_b", KC * CH)
    val_b = SA("val_b", 5 * C)
    q_b = SA("q_b", 2 * TCOL)
    p_b = SA("p_b", 2 * TCOL)
    r_b = SA("r_b", 2 * TCOL)
    st_b = SA("st_b", 2 * TCOL)
    bv_b = SA("bv_b", 2 * TCOL)
    wbufA = SA("wbufA", 2 * KC * 128)
    wbufB = SA("wbufB", 2 * C)
    mast_b = SA("mast_b", 143)
    mast_d = SA("mast_d", 143)
    iden_s = SA("iden_s", 128)
    ones_col = SA("ones_col_s", 1)
    ones_col_r = nc.alloc_sbuf_tensor("ones_col_r", [128, 1], F32R).ap()
    cc1 = SA("cc1", 32)
    cs1 = SA("cs1", 32)
    b1s = SA("b1s", 32)
    bdisps = SA("bdisps", 2)
    bvalr = SA("bvalr", C)
    bstater = SA("bstater", 1)
    bstrainr = SA("bstrainr", 1)
    bdmgr = SA("bdmgr", 1)
    bdmgor = SA("bdmgor", 1)
    hpi_col = SA("hpi_col_s", 1)
    e_rep = SA("e_rep", 32)
    qmask_s = SA("qmask_s", HALO)
    expmask_s = SA("expmask_s", 62)
    tmp32 = SA("tmp32", 32)
    tmp32b = SA("tmp32b", 32)
    sc32 = SA("sc32", 32)
    msk32 = SA("msk32", 32)
    wpos_s = SA("wpos_s", 16, parts=16)
    relpe_s = SA("relpe_s", 32, parts=16)
    rowpool = SA("rowpool", 8064)
    R_MEAN, R_NEGM, R_A, R_TMP = 0, CH, 2 * CH, 3 * CH
    CSW1_O, CSVAL_O, CSDISP_O = 4 * CH, 4 * CH + DFF, 4 * CH + DFF + C
    R_MA, R_ONES = CSDISP_O + 256, CSDISP_O + 256 + CH

    def rrow(off, n):
        return rowpool[0:1, off:off + n]

    def r32(off, n):
        return rowpool[32:33, off:off + n]
    a2col = SA("a2col", NJT + 1)
    sums = SA("sums", 48)
    ma_bc = SA("ma_bc", CH)
    b2col = SA("b2col", 8)
    bcpcol = SA("bcpcol", 8)
    csw1c = SA("csw1c", 32)
    csdispc = SA("csdispc", 2)

    ARENA = 18432
    arena = SA("arena", ARENA)
    arena_r = nc.alloc_sbuf_tensor_at(
        "arena_r", [128, ARENA], F32R,
        offset=nc.lookup_mloc(sb["arena"]).addr).ap()

    def ar_r(off, n):
        return arena_r[:, off:off + n]
    O_PRE_B = 0
    O_PRE_D = 2048
    O_ACT_B = 4096
    O_ACT_D = 6144
    O_DISP = 8192
    O_SQ = 9344
    O_SIG = 8192
    O_ESTK = 10496
    O_WSH = 12544
    O_WBL = 13056
    O_ATT = 13568
    O_SCR = 16384
    O_STG = 15616
    O_WTM = 0
    O_WTM1 = 5120
    O_WSHX = 4096
    O_STG4 = 14336

    def h_t(i):
        return arena[:, i * CH:(i + 1) * CH]

    def ar(off, n):
        return arena[:, off:off + n]

    def artok(off, n):
        return ["ar:%d" % b for b in range(off // 512, (off + n + 511) // 512)]

    wtm2 = [ar(O_WTM, SROW), ar(O_WTM1, SROW)]
    WTM_T2 = [artok(O_WTM, SROW), artok(O_WTM1, SROW)]
    a_bc = SA("a_bc", CH)
    ABC_T = ["a_bc"]
    stg = ar(O_STG, 2 * CH)
    disp_b = ar(O_DISP, 2 * TCOL)
    DISP_T = artok(O_DISP, 2 * TCOL)
    sq_b = ar(O_SQ, 2 * TCOL)
    SQ_T = artok(O_SQ, 2 * TCOL)
    invc_b = ar(O_ESTK, TCOL - 31)
    INVC_T = artok(O_ESTK, TCOL - 31)

    def xh_of(kk):
        return x_b[:, kk * 128:(kk + 1) * 128]

    def x1h_of(kk):
        return x1_b[:, kk * 128:(kk + 1) * 128]

    def XT(kk):
        return "x%d" % kk

    def X1T(kk):
        return "x1_%d" % kk

    XALL = [XT(k) for k in range(KC)]
    X1ALL = [X1T(k) for k in range(KC)]

    ps = [nc.alloc_psum_tensor("ps%d" % i, [128, 512], F32).ap() for i in range(8)]
    PT = ["ps%d" % i for i in range(8)]

    S = Sched()

    def dma(engine, dst, src, reads=(), writes=(), always_inc=False, sem_key=None,
            slow=False):
        return S.op(engine,
                    lambda e, d_=dst, s_=src, sl=slow: e.dma_start(
                        d_, s_, allow_slow_non_contiguous=sl),
                    reads=reads, writes=writes, always_inc=always_inc,
                    sem_key=sem_key)

    def pe_bcast(row_ap, n, dst_ap, psb, reads, writes):
        top(lambda e, row_ap=row_ap, n=n, psb=psb: e.matmul(
            ps[psb][:, 0:n], rrow(R_ONES, 128), row_ap, start=True, stop=True),
            reads=["rp_ones"] + list(reads), writes=[PT[psb]])
        vop(lambda e, dst_ap=dst_ap, n=n, psb=psb: e.tensor_copy(
            dst_ap, ps[psb][:, 0:n]),
            reads=[PT[psb]], writes=writes)

    _bounce_ctr = [0]

    def bcast_row(row_ap, n, dst_ap, reads, writes):
        slot = _bounce_ctr[0] % 4
        _bounce_ctr[0] += 1
        off = slot * 1024
        tok = "rsc%d" % slot
        if n > 1024:
            off, tok = 4096, "rsc4"
        dma("gpsimd", AP(rowscr, off, [[1, 1], [1, n]]), row_ap,
            reads=reads, writes=[tok])
        dma("gpsimd", dst_ap,
            AP(rowscr, off, [[0, 128], [1, n]]), reads=[tok], writes=writes)

    def spread_row(row_ap, n, dst_ap, reads, writes):
        slot = _bounce_ctr[0] % 4
        _bounce_ctr[0] += 1
        off = slot * 1024
        tok = "rsc%d" % slot
        if n > 1024:
            off, tok = 4096, "rsc4"
        dma("gpsimd", AP(rowscr, off, [[1, 1], [1, n]]), row_ap,
            reads=reads, writes=[tok])
        if n <= 128:
            dma("gpsimd", dst_ap, AP(rowscr, off, [[1, n], [1, 1]]),
                reads=[tok], writes=writes)
        else:
            dma("gpsimd", dst_ap, AP(rowscr, off, [[1, 128], [128, n // 128]]),
                reads=[tok], writes=writes, slow=True)

    def top(fn, reads=(), writes=()):
        return S.op("tensor", fn, reads=reads, writes=writes)

    def vop(fn, reads=(), writes=()):
        return S.op("vector", fn, reads=reads, writes=writes)

    def aop(fn, reads=(), writes=()):
        return S.op("scalar", fn, reads=reads, writes=writes)

    def wbt(lo, n):
        return ["wBg%d" % g for g in range(lo // 512, (lo + n + 511) // 512)]

    def wAtok(buf, h):
        return "wA%d_%d" % (buf, h)

    def _load_wA(src, i):
        buf = i % 2
        wdst = wbufA[:, buf * KC * 128:(buf + 1) * KC * 128]
        dma("sync", wdst, src[i],
            writes=[wAtok(buf, q) for q in range(4)])

    def load_w1(m):
        _load_wA(w1re, m)

    def load_wcp(mo):
        _load_wA(wcpre, mo)

    for (dst, key, tok) in [
        (ones_col, "ones_col", "onesc"), (iden_s, "iden", "iden"),
        (b2col, "b2_t", "b2col"),
        (tmp32, "theta_t", "tmp32"), (cc1, "wreal_t", "cc1"),
        (cs1, "wimag_t", "cs1"), (b1s, "b1_t", "b1s"),
        (hpi_col, "hpi_col", "hpi"),
    ]:
        dma("sync", dst, din[key].ap(), writes=[tok])
    for kk in range(KC):
        dma("sync", x_b[:, kk * 128:(kk + 1) * 128],
            xT[kk * 128:(kk + 1) * 128, 0:128], writes=[XT(kk // 4)])
    load_w1(0)
    load_w1(1)
    for (dst, key, tok) in [
        (bcpcol, "bcproj_t", "bcpcol"),
        (mast_b, "master_bond", "mastb"), (mast_d, "master_dmg", "mastd"),
        (bdisps, "bdisp_t", "bdisps"),
        (bvalr, "bval_rep", "bvalr"),
        (bstater, "bstate_rep", "bstater"), (bstrainr, "bstrain_rep", "bstrainr"),
        (bdmgr, "bdmg_rep", "bdmgr"), (bdmgor, "bdmgout_rep", "bdmgor"),
        (qmask_s, "qmask", "qmask"), (expmask_s, "expmask", "expmask"),
        (wpos_s, "wpos", "wpos"),
        (relpe_s, "relpeT", "relpe"),
    ]:
        dma("sync", dst, din[key].ap(), writes=[tok], sem_key="d:cst")

    vop(lambda e: e.tensor_copy(ones_col_r, ones_col), reads=["onesc"],
        writes=["onescr"])
    vop(lambda e: e.memset(rowpool[64:65, 0:128], 0.0), writes=["zrow"])
    vop(lambda e: e.memset(rrow(R_ONES, 128), 1.0), writes=["rp_ones"])
    vop(lambda e: e.memset(val_b[:, 0:C], 0.0), writes=["val0"])
    for reg in range(2):
        dma("gpsimd", AP(scratch, reg * SREG, [[1, 97]]),
            rowpool[64:65, 0:97], reads=["zrow"], writes=["sczero%d" % reg])

    def ln_stats(x_of_k, xtok, ncols):
        for kk in range(KC):
            xr = ar_r(O_SCR + 1024 + (kk % 2) * CH, ncols)
            xrt = artok(O_SCR + 1024 + (kk % 2) * CH, ncols)
            sqv = ar_r(O_SCR + (kk % 2) * CH, ncols)
            stk = artok(O_SCR + (kk % 2) * CH, ncols)
            vop(lambda e, xr=xr, kk=kk: e.tensor_copy(xr, x_of_k(kk)),
                reads=xtok(kk), writes=xrt)
            top(lambda e, xr=xr, kk=kk: e.matmul(ps[2][0:1, 0:ncols], ones_col_r, xr,
                                                 start=(kk == 0), stop=(kk == KC - 1)),
                reads=["onescr"] + xrt, writes=["ps2"])
            aop(lambda e, sqv=sqv, kk=kk: e.activation(sqv, x_of_k(kk), AF.Square),
                reads=xtok(kk), writes=stk)
            top(lambda e, sqv=sqv, kk=kk: e.matmul(ps[3][0:1, 0:ncols], ones_col_r,
                                                   sqv,
                                                   start=(kk == 0), stop=(kk == KC - 1)),
                reads=["onescr"] + stk, writes=["ps3"])
        rm = rrow(R_MEAN, ncols)
        rn = rrow(R_NEGM, ncols)
        ra = rrow(R_A, ncols)
        rt = rrow(R_TMP, ncols)
        vop(lambda e: e.tensor_scalar_mul(rm, ps[2][0:1, 0:ncols], 1.0 / C),
            reads=["ps2"], writes=["r_mean"])
        vop(lambda e: e.tensor_scalar_mul(rn, rm, -1.0), reads=["r_mean"],
            writes=["r_negm"])
        vop(lambda e: e.tensor_mul(rt, rm, rm), reads=["r_mean"], writes=["r_tmp"])
        vop(lambda e: e.scalar_tensor_tensor(rt, ps[3][0:1, 0:ncols], 1.0 / C, rt,
                                             AL.mult, AL.subtract),
            reads=["ps3", "r_tmp"], writes=["r_tmp"])
        vop(lambda e: e.tensor_scalar_add(rt, rt, EPS), reads=["r_tmp"],
            writes=["r_tmp"])
        aop(lambda e: e.activation(ra, rt, AF.Ln), reads=["r_tmp"], writes=["r_a"])
        aop(lambda e: e.activation(ra, ra, AF.Exp, scale=-0.5), reads=["r_a"],
            writes=["r_a"])
        vop(lambda e: e.tensor_mul(rm, rt, ra), reads=["r_tmp", "r_a"],
            writes=["r_mean"])
        vop(lambda e: e.tensor_mul(rm, rm, ra), reads=["r_mean", "r_a"],
            writes=["r_mean"])
        vop(lambda e: e.tensor_scalar(rm, rm, -0.5, 1.5, AL.mult, AL.add),
            reads=["r_mean"], writes=["r_mean"])
        vop(lambda e: e.tensor_mul(ra, ra, rm), reads=["r_a", "r_mean"],
            writes=["r_a"])
        vop(lambda e: e.tensor_mul(rrow(R_MA, ncols), rn, ra),
            reads=["r_negm", "r_a"], writes=["r_ma"])

    def ffn_pass(x_of_k, xtok, ncols, compute_cs, post_evac,
                 skip_w1=0, after_w1=None, skip_stats=False):
        if not skip_stats:
            ln_stats(x_of_k, xtok, ncols)
        for m in range(MT1):
            buf = m % 2
            wdst = wbufA[:, buf * KC * 128:(buf + 1) * KC * 128]
            if m >= skip_w1:
                load_w1(m)
            pb = ps[m % 2][:, 0:ncols]
            pt = PT[m % 2]
            for kk in range(KC):
                top(lambda e, kk=kk, pb=pb, wdst=wdst: e.matmul(
                    pb, wdst[:, kk * 128:(kk + 1) * 128], x_of_k(kk),
                    start=(kk == 0), stop=(kk == KC - 1 and not compute_cs)),
                    reads=[wAtok(buf, kk // 2)] + xtok(kk), writes=[pt])
            if m == 0:
                pe_bcast(rrow(R_A, ncols), ncols, a_bc[:, 0:ncols], 5,
                         ["r_a"], ABC_T)
                pe_bcast(rrow(R_MA, ncols), ncols, ma_bc[:, 0:ncols], 6,
                         ["r_ma"], ["ma_bc"])
            hT = h_t(m)[:, 0:ncols]
            htok = artok(m * CH, CH)
            if compute_cs:
                for kk in range(KC):
                    top(lambda e, kk=kk, wdst=wdst: e.matmul(
                        ps[4][0:1, 0:128], ones_col, wdst[:, kk * 128:(kk + 1) * 128],
                        start=(kk == 0), stop=(kk == KC - 1)),
                        reads=["onesc", wAtok(buf, kk // 2)], writes=["ps4"])
                aop(lambda e, m=m: e.copy(rrow(CSW1_O + m * 128, 128),
                                          ps[4][0:1, 0:128]),
                    reads=["ps4"], writes=["cs_w1_%d" % m])
                top(lambda e, m=m, pb=pb: e.matmul(
                    pb, rrow(CSW1_O + m * 128, 128), rrow(R_NEGM, ncols),
                    start=False, stop=True),
                    reads=["cs_w1_%d" % m, "r_negm"], writes=[pt])
                vop(lambda e, hT=hT, pb=pb: e.tensor_mul(hT, pb, a_bc[:, 0:ncols]),
                    reads=[pt] + ABC_T, writes=htok)
            else:
                vop(lambda e, hT=hT, pb=pb: e.tensor_mul(hT, pb, a_bc[:, 0:ncols]),
                    reads=[pt] + ABC_T, writes=htok)
                vop(lambda e, hT=hT, m=m: e.scalar_tensor_tensor(
                    hT, ma_bc[:, 0:ncols], csw1c[:, m:m + 1], hT, AL.mult, AL.add),
                    reads=htok + ["ma_bc", "csw1c"], writes=htok)
            sA = ar(O_SCR + 1024 + (m % 2) * CH, ncols)
            sAt = artok(O_SCR + 1024 + (m % 2) * CH, ncols)
            sB = ar(O_SCR + (m % 2) * CH, ncols)
            sBt = artok(O_SCR + (m % 2) * CH, ncols)
            aop(lambda e, hT=hT, sA=sA: e.activation(sA, hT, AF.Sin, scale=0.1),
                reads=htok, writes=sAt)
            aop(lambda e, hT=hT, sB=sB: e.activation(sB, hT, AF.Sin, bias=hpi_col,
                                                     scale=0.1),
                reads=htok + ["hpi"], writes=sBt)
            vop(lambda e, hT=hT, sA=sA, m=m: e.scalar_tensor_tensor(
                hT, sA, cs1[:, m:m + 1], hT, AL.mult, AL.add),
                reads=sAt + htok + ["cs1"], writes=htok)
            vop(lambda e, hT=hT, sB=sB, m=m: e.scalar_tensor_tensor(
                hT, sB, cc1[:, m:m + 1], hT, AL.mult, AL.add),
                reads=sBt + htok + ["cc1"], writes=htok)
            vop(lambda e, hT=hT, m=m: e.tensor_scalar(
                hT, hT, b1s[:, m:m + 1], 0.0, AL.add, AL.max),
                reads=htok + ["b1s"], writes=htok)
        if after_w1 is not None:
            after_w1()
        for ki in range(MT1):
            buf = ki % 2
            wdst = wbufB[:, buf * C:(buf + 1) * C]
            dma("sync", wdst, w2[ki * 128:(ki + 1) * 128, :],
                writes=wbt(buf * C, 1024))
            for mo in range(KC):
                top(lambda e, ki=ki, mo=mo, wdst=wdst: e.matmul(
                    ps[mo][:, 0:ncols], wdst[:, mo * 128:(mo + 1) * 128],
                    h_t(ki)[:, 0:ncols], start=(ki == 0), stop=(ki == MT1 - 1)),
                    reads=wbt(buf * C + mo * 128, 128) + artok(ki * CH, CH),
                    writes=[PT[mo]])
        for mo in range(KC):
            post_evac(mo, ps[mo][:, 0:ncols], PT[mo])

    ln_stats(xh_of, lambda kk: [XT(kk // 4)], 128)

    top(lambda e: e.matmul(ps[0][0:16, 0:32], wpos_s, relpe_s, start=True, stop=True),
        reads=["wpos", "relpe"], writes=["ps0"])
    aop(lambda e: e.copy(e_rep[0:16, :], ps[0][0:16, 0:32]),
        reads=["ps0"], writes=["e_rep"])
    for kk in range(1, 8):
        dma("gpsimd", e_rep[16 * kk:16 * (kk + 1), :], e_rep[0:16, :],
            reads=["e_rep"], writes=["e_rep_c%d" % kk])
    vop(lambda e: e.tensor_scalar_add(e_rep, e_rep, bstrainr),
        reads=["e_rep", "bstrainr"] + ["e_rep_c%d" % k for k in range(1, 8)],
        writes=["e_rep"])

    q4 = ar(O_SCR, 128)
    m4 = ar(O_SCR + 128, 128)
    QT = artok(O_SCR, 256)
    vop(lambda e: e.tensor_copy(q4[:, 0:32], tmp32), reads=["tmp32"], writes=QT)
    vop(lambda e: e.tensor_scalar_add(q4[:, 32:64], tmp32, HPI),
        reads=["tmp32"] + QT, writes=QT)
    vop(lambda e: e.scalar_tensor_tensor(q4[:, 64:96], b1s, 0.1, tmp32,
                                         AL.mult, AL.add),
        reads=["b1s", "tmp32"] + QT, writes=QT)
    vop(lambda e: e.tensor_scalar_add(q4[:, 96:128], q4[:, 64:96], HPI),
        reads=QT, writes=QT)
    for _ in range(7):
        vop(lambda e: e.tensor_single_scalar(m4, q4, PI, AL.is_gt),
            reads=QT, writes=QT)
        vop(lambda e: e.scalar_tensor_tensor(q4, m4, -TWOPI, q4,
                                             AL.mult, AL.add),
            reads=QT, writes=QT)
        vop(lambda e: e.tensor_single_scalar(m4, q4, -PI, AL.is_lt),
            reads=QT, writes=QT)
        vop(lambda e: e.scalar_tensor_tensor(q4, m4, TWOPI, q4,
                                             AL.mult, AL.add),
            reads=QT, writes=QT)
    aop(lambda e: e.activation(q4, q4, AF.Sin), reads=QT, writes=QT)
    vop(lambda e: e.tensor_mul(tmp32b, cs1, q4[:, 0:32]),
        reads=["cs1"] + QT, writes=["tmp32b"])
    vop(lambda e: e.tensor_mul(cc1, cc1, q4[:, 32:64]),
        reads=["cc1"] + QT, writes=["cc1"])
    vop(lambda e: e.tensor_add(cc1, cc1, tmp32b), reads=["cc1", "tmp32b"],
        writes=["cc1"])
    vop(lambda e: e.tensor_mul(cc1, cc1, cs1), reads=["cc1", "cs1"], writes=["cc1"])
    vop(lambda e: e.tensor_scalar_mul(cc1, cc1, 0.1), reads=["cc1"], writes=["cc1"])
    vop(lambda e: e.tensor_mul(cs1, cc1, q4[:, 96:128]),
        reads=["cc1"] + QT, writes=["cs1"])
    vop(lambda e: e.tensor_mul(cc1, cc1, q4[:, 64:96]),
        reads=["cc1"] + QT, writes=["cc1"])


    def boot_evac(mo, pb, pt):
        vop(lambda e, mo=mo, pb=pb: e.tensor_scalar(
            x1h_of(mo), pb, b2col[:, mo:mo + 1], 0.5, AL.add, AL.mult),
            reads=[pt, "b2col"], writes=[X1T(mo // 4)])
        vop(lambda e, mo=mo: e.tensor_add(x1h_of(mo), x1h_of(mo), xh_of(mo)),
            reads=[X1T(mo // 4), XT(mo // 4)], writes=[X1T(mo // 4)])

    ffn_pass(xh_of, lambda kk: [XT(kk // 4)], 128, True, boot_evac,
             skip_w1=2, skip_stats=True,
             after_w1=lambda: (load_w1(0), load_w1(1)))

    ln_stats(x1h_of, lambda kk: [X1T(kk // 4)], 128)
    pe_bcast(rrow(R_A, 128), 128, a_bc[:, 0:128], 5, ["r_a"], ABC_T)
    pe_bcast(rrow(R_MA, 128), 128, ma_bc[:, 0:128], 6, ["r_ma"], ["ma_bc"])
    spread_row(rrow(R_A, 128), 128, a2col[:, 4:5], ["r_a"], ["a2c4"])
    spread_row(rrow(CSW1_O, DFF), DFF, csw1c, ["cs_w1_%d" % m for m in range(MT1)],
               ["csw1c"])

    def load_wval(kk):
        buf = kk % 2
        wdst = wbufB[:, buf * C:(buf + 1) * C]
        dma("sync", wdst, wval[kk * 128:(kk + 1) * 128, :],
            writes=wbt(buf * C, 1024))

    for kk in range(KC):
        buf = kk % 2
        wdst = wbufB[:, buf * C:(buf + 1) * C]
        load_wval(kk)
        for n in range(2):
            top(lambda e, kk=kk, n=n, wdst=wdst: e.matmul(
                ps[n][:, 0:512], x1h_of(kk), wdst[:, n * 512:(n + 1) * 512],
                start=(kk == 0), stop=False),
                reads=wbt(buf * C + n * 512, 512) + [X1T(kk // 4)], writes=[PT[n]])
            top(lambda e, kk=kk, n=n, wdst=wdst: e.matmul(
                ps[4 + n][0:1, 0:512], ones_col, wdst[:, n * 512:(n + 1) * 512],
                start=(kk == 0), stop=(kk == KC - 1)),
                reads=["onesc"] + wbt(buf * C + n * 512, 512), writes=[PT[4 + n]])
    for n in range(2):
        aop(lambda e, n=n: e.copy(rrow(CSVAL_O + n * 512, 512),
                                  ps[4 + n][0:1, 0:512]),
            reads=[PT[4 + n]], writes=["cs_val%d" % n])
    for n in range(2):
        top(lambda e, n=n: e.matmul(
            ps[n][:, 0:512], rrow(R_NEGM, 128), rrow(CSVAL_O + n * 512, 512),
            start=False, stop=True),
            reads=["r_negm", "cs_val%d" % n], writes=[PT[n]])
        vop(lambda e, n=n: e.scalar_tensor_tensor(
            val_b[:, n * 512:(n + 1) * 512], ps[n][:, 0:512], a2col[:, 4:5],
            bvalr[:, n * 512:(n + 1) * 512], AL.mult, AL.add),
            reads=[PT[n], "a2c4", "bvalr"], writes=["val0"])

    def load_wdisp():
        dma("sync", wbufB, wdispT.ap(), writes=wbt(0, 2048))

    def wdisp_sl(kk, mi):
        return wbufB[:, kk * 256 + mi * 128: kk * 256 + (mi + 1) * 128]

    load_wdisp()
    for kk in range(KC):
        top(lambda e, kk=kk: e.matmul(ps[2][0:1, 0:256], ones_col,
                                      wbufB[:, kk * 256:(kk + 1) * 256],
                                      start=(kk == 0), stop=(kk == KC - 1)),
            reads=["onesc"] + wbt(kk * 256, 256), writes=["ps2"])
    aop(lambda e: e.copy(rrow(CSDISP_O, 256), ps[2][0:1, 0:256]),
        reads=["ps2"], writes=["cs_disp"])
    spread_row(rrow(CSDISP_O, 256), 256, csdispc, ["cs_disp"], ["csdispc"])

    for mi in range(2):
        pb = ps[2 + mi][:, 0:HALO]
        pt = PT[2 + mi]
        for kk in range(KC):
            top(lambda e, kk=kk, mi=mi, pb=pb: e.matmul(
                pb, wdisp_sl(kk, mi), x1h_of(kk)[:, 66:128],
                start=(kk == 0), stop=(kk == KC - 1)),
                reads=wbt(kk * 256 + mi * 128, 128) + [X1T(kk // 4)], writes=[pt])
        vop(lambda e, mi=mi, pb=pb: e.tensor_mul(
            disp_b[:, mi * TCOL: mi * TCOL + HALO], pb, a_bc[:, 66:128]),
            reads=[pt] + ABC_T, writes=DISP_T)
        vop(lambda e, mi=mi: e.scalar_tensor_tensor(
            disp_b[:, mi * TCOL: mi * TCOL + HALO], ma_bc[:, 66:128],
            csdispc[:, mi:mi + 1], disp_b[:, mi * TCOL: mi * TCOL + HALO],
            AL.mult, AL.add),
            reads=DISP_T + ["ma_bc", "csdispc"], writes=DISP_T)
        vop(lambda e, mi=mi: e.tensor_scalar_add(
            disp_b[:, mi * TCOL: mi * TCOL + HALO],
            disp_b[:, mi * TCOL: mi * TCOL + HALO], bdisps[:, mi:mi + 1]),
            reads=DISP_T + ["bdisps"], writes=DISP_T)

    BD3 = {"bd_state": 0, "bd_strain": 512, "bd_damage": 1024}
    BD2 = {"bd_si": 0, "bd_sj": 512}

    def load_bd3():
        for nm, off in BD3.items():
            dma("sync", wbufB[:, off:off + 512], din[nm].ap(),
                writes=wbt(off, 512))

    def load_bd2():
        for nm, off in BD2.items():
            dma("sync", wbufB[:, off:off + 512], din[nm].ap(),
                writes=wbt(off, 512))

    def bd_sl(base, kk, mi):
        return wbufB[:, base + kk * 256 + mi * 128: base + kk * 256 + (mi + 1) * 128]

    load_bd3()
    for (buf, btok, nm) in ((q_b, "q", "bd_state"), (p_b, "p", "bd_strain"),
                            (r_b, "r", "bd_damage")):
        for mi in range(2):
            pb = ps[4 + mi][:, 0:HALO]
            pt = PT[4 + mi]
            for kk in range(2):
                top(lambda e, kk=kk, mi=mi, pb=pb, nm=nm: e.matmul(
                    pb, bd_sl(BD3[nm], kk, mi),
                    disp_b[:, kk * TCOL: kk * TCOL + HALO],
                    start=(kk == 0), stop=(kk == 1)),
                    reads=wbt(BD3[nm] + kk * 256 + mi * 128, 128) + DISP_T,
                    writes=[pt])
            aop(lambda e, buf=buf, mi=mi, pb=pb: e.copy(
                buf[:, mi * TCOL: mi * TCOL + HALO], pb),
                reads=[pt], writes=[btok])
    vop(lambda e: e.tensor_mul(
        q_b.rearrange("p (h t) -> p h t", h=2)[:, :, 0:HALO],
        q_b.rearrange("p (h t) -> p h t", h=2)[:, :, 0:HALO],
        qmask_s[:, None, :].broadcast_to([128, 2, HALO])),
        reads=["q", "qmask"], writes=["q"])

    def x_k(kk):
        return x_b[:, kk * CH:(kk + 1) * CH]

    def x1_k(kk):
        return x1_b[:, kk * CH:(kk + 1) * CH]

    def sh3(buf, off, n, shift=0):
        return buf.rearrange("p (h t) -> p h t", h=2)[
            :, :, off + shift: off + shift + n]

    def sh3a(off0, off, n, shift=0):
        return arena[:, off0:off0 + 2 * TCOL].rearrange(
            "p (h t) -> p h t", h=2)[:, :, off + shift: off + shift + n]

    def emit_sigs(c):
        order = (0, 2, 1, 3)
        for i in order:
            sig = ar(O_SIG + i * CH, CH)
            sigt = artok(O_SIG + i * CH, CH)
            aop(lambda e, i=i, sig=sig: e.activation(
                sig, ps[4 + i][:, 0:CH], AF.Sigmoid, bias=bdmgor),
                reads=[PT[4 + i], "bdmgor"], writes=sigt)
        for i in order:
            sig = ar(O_SIG + i * CH, CH)
            sigt = artok(O_SIG + i * CH, CH)
            est = ar(O_ESTK + i * CH, CH)
            estt = artok(O_ESTK + i * CH, CH)
            vop(lambda e, i=i, sig=sig, est=est: e.scalar_tensor_tensor(
                est, sig, -10.0, ps[i][:, 0:CH], AL.mult, AL.add),
                reads=sigt + [PT[i]], writes=estt)
        for i in order:
            est = ar(O_ESTK + i * CH, CH)
            estt = artok(O_ESTK + i * CH, CH)
            aop(lambda e, est=est: e.activation(est, est, AF.Exp),
                reads=estt, writes=estt)
        if c == 0:
            for i in order:
                est = ar(O_ESTK + i * CH, CH)
                estt = artok(O_ESTK + i * CH, CH)
                ab = i % 2
                vop(lambda e, est=est, ab=ab: e.tensor_mul(
                    est[:, 0:31], est[:, 0:31],
                    expmask_s[:, ab * 31:(ab + 1) * 31]),
                    reads=estt + ["expmask"], writes=estt)

    for c in range(NCHUNK):
        if c > 0:
            for (buf, btok) in ((q_b, "q"), (p_b, "p"), (r_b, "r")):
                vop(lambda e, buf=buf: e.tensor_copy(
                    sh3(buf, 0, HALO), sh3(buf, CH, HALO)),
                    reads=[btok], writes=[btok])
            vop(lambda e: e.tensor_copy(val_b[:, 0:C], val_b[:, 4 * C:5 * C]),
                reads=["val4"], writes=["val0"])

        for kk in range(KC):
            dma("sync", x_b[:, kk * CH:(kk + 1) * CH],
                xT[kk * 128:(kk + 1) * 128, 128 + c * CH:128 + (c + 1) * CH],
                writes=[XT(kk)])
        def ffn1_evac(mo, pb, pt):
            vop(lambda e, mo=mo, pb=pb: e.tensor_scalar(
                x1_k(mo), pb, b2col[:, mo:mo + 1], 0.5, AL.add, AL.mult),
                reads=[pt, "b2col"], writes=[X1T(mo)])
            vop(lambda e, mo=mo: e.tensor_add(x1_k(mo), x1_k(mo), x_k(mo)),
                reads=[X1T(mo), XT(mo)], writes=[X1T(mo)])

        ffn_pass(x_k, lambda kk: [XT(kk)], CH, False, ffn1_evac, skip_w1=2)

        dma("gpsimd", invc_b, din["invc"][c], writes=INVC_T, sem_key="d:invc")
        ln_stats(x1_k, lambda kk: [X1T(kk)], CH)

        load_wdisp()
        for mi in range(2):
            pb = ps[mi][:, 0:CH]
            pt = PT[mi]
            for kk in range(KC):
                top(lambda e, kk=kk, mi=mi, pb=pb: e.matmul(
                    pb, wdisp_sl(kk, mi), x1_k(kk),
                    start=(kk == 0), stop=(kk == KC - 1)),
                    reads=wbt(kk * 256 + mi * 128, 128) + [X1T(kk)], writes=[pt])
            if mi == 0:
                pe_bcast(rrow(R_A, CH), CH, a_bc, 5, ["r_a"], ABC_T)
                pe_bcast(rrow(R_MA, CH), CH, ma_bc, 6, ["r_ma"], ["ma_bc"])
                for j in range(NJT):
                    spread_row(rrow(R_A + j * 128, 128), 128, a2col[:, j:j + 1],
                               ["r_a"], ["a2c%d" % j])
            vop(lambda e, mi=mi, pb=pb: e.tensor_mul(
                disp_b[:, mi * TCOL + HALO:(mi + 1) * TCOL], pb, a_bc),
                reads=[pt] + ABC_T, writes=DISP_T)
            vop(lambda e, mi=mi: e.scalar_tensor_tensor(
                disp_b[:, mi * TCOL + HALO:(mi + 1) * TCOL], ma_bc,
                csdispc[:, mi:mi + 1],
                disp_b[:, mi * TCOL + HALO:(mi + 1) * TCOL], AL.mult, AL.add),
                reads=DISP_T + ["ma_bc", "csdispc"], writes=DISP_T)
            vop(lambda e, mi=mi: e.tensor_scalar_add(
                disp_b[:, mi * TCOL + HALO:(mi + 1) * TCOL],
                disp_b[:, mi * TCOL + HALO:(mi + 1) * TCOL], bdisps[:, mi:mi + 1]),
                reads=DISP_T + ["bdisps"], writes=DISP_T)

        load_bd3()
        for (buf, btok, nm) in ((q_b, "q", "bd_state"), (p_b, "p", "bd_strain"),
                                (r_b, "r", "bd_damage")):
            for mi in range(2):
                pb = ps[2 + mi][:, 0:CH]
                pt = PT[2 + mi]
                for kk in range(2):
                    top(lambda e, kk=kk, mi=mi, pb=pb, nm=nm: e.matmul(
                        pb, bd_sl(BD3[nm], kk, mi),
                        disp_b[:, kk * TCOL + HALO:(kk + 1) * TCOL],
                        start=(kk == 0), stop=(kk == 1)),
                        reads=wbt(BD3[nm] + kk * 256 + mi * 128, 128) + DISP_T,
                        writes=[pt])
                aop(lambda e, buf=buf, mi=mi, pb=pb: e.copy(
                    buf[:, mi * TCOL + HALO:(mi + 1) * TCOL], pb),
                    reads=[pt], writes=[btok])

        seq = [(q_b, ["q"]), (sq_b, SQ_T), (st_b, ["st"]),
               (sq_b, SQ_T), (st_b, ["st"]), (sq_b, SQ_T)]
        for i, sft in enumerate((1, 2, 4, 8, 16)):
            off = 2 * sft - 1
            nn = TCOL - off
            sbuf, srd = seq[i]
            dbuf, dwr = seq[i + 1]
            vop(lambda e, sbuf=sbuf, dbuf=dbuf, sft=sft, nn=nn, off=off:
                e.tensor_add(sh3(dbuf, off, nn), sh3(sbuf, off, nn),
                             sh3(sbuf, off, nn, -sft)),
                reads=srd, writes=dwr)
        nn = TCOL - 31
        vop(lambda e, nn=nn: e.tensor_mul(
            sh3(sq_b, 31, nn), sh3(sq_b, 31, nn),
            invc_b[:, None, :].broadcast_to([128, 2, nn])),
            reads=SQ_T + INVC_T, writes=SQ_T)
        vop(lambda e, nn=nn: e.tensor_sub(
            sh3(st_b, 31, nn), sh3(sq_b, 31, nn), sh3(q_b, 31, nn)),
            reads=SQ_T + ["q"], writes=["st"])
        vop(lambda e, nn=nn: e.tensor_scalar_add(
            sh3(st_b, 31, nn), sh3(st_b, 31, nn), bstater),
            reads=["st", "bstater"], writes=["st"])

        for kk in range(KC):
            buf = kk % 2
            wdst = wbufB[:, buf * C:(buf + 1) * C]
            load_wval(kk)
            for j in range(NJT):
                for n in range(2):
                    top(lambda e, kk=kk, n=n, j=j, wdst=wdst: e.matmul(
                        ps[2 * j + n][:, 0:512],
                        x1_b[:, kk * CH + j * 128: kk * CH + (j + 1) * 128],
                        wdst[:, n * 512:(n + 1) * 512],
                        start=(kk == 0), stop=False),
                        reads=wbt(buf * C + n * 512, 512) + [X1T(kk)],
                        writes=[PT[2 * j + n]])
        for j in range(NJT):
            for n in range(2):
                top(lambda e, n=n, j=j: e.matmul(
                    ps[2 * j + n][:, 0:512], rrow(R_NEGM + j * 128, 128),
                    rrow(CSVAL_O + n * 512, 512), start=False, stop=True),
                    reads=["r_negm", "cs_val%d" % n], writes=[PT[2 * j + n]])
                vop(lambda e, n=n, j=j: e.scalar_tensor_tensor(
                    val_b[:, (1 + j) * C + n * 512:(1 + j) * C + (n + 1) * 512],
                    ps[2 * j + n][:, 0:512], a2col[:, j:j + 1],
                    bvalr[:, n * 512:(n + 1) * 512], AL.mult, AL.add),
                    reads=[PT[2 * j + n], "a2c%d" % j, "bvalr"],
                    writes=["val%d" % (1 + j)])

        load_bd2()
        for mi in range(2):
            pb = ps[4 + mi][:, 0:CH]
            pt = PT[4 + mi]
            for kk in range(2):
                top(lambda e, kk=kk, mi=mi, pb=pb: e.matmul(
                    pb, bd_sl(BD2["bd_si"], kk, mi),
                    st_b[:, kk * TCOL + HALO:(kk + 1) * TCOL],
                    start=(kk == 0), stop=(kk == 1)),
                    reads=wbt(BD2["bd_si"] + kk * 256 + mi * 128, 128) + ["st"],
                    writes=[pt])
            vop(lambda e, mi=mi, pb=pb: e.tensor_sub(
                bv_b[:, mi * TCOL + HALO:(mi + 1) * TCOL], pb,
                p_b[:, mi * TCOL + HALO:(mi + 1) * TCOL]),
                reads=[pt, "p"], writes=["bv"])
        for (c0, cn, pbi) in ((HALO, CH, 6), (31, 31, 7)):
            for mi in range(2):
                pb = ps[pbi][:, 0:cn]
                pt = PT[pbi]
                for kk in range(2):
                    top(lambda e, kk=kk, mi=mi, pb=pb, c0=c0, cn=cn: e.matmul(
                        pb, bd_sl(BD2["bd_sj"], kk, mi),
                        st_b[:, kk * TCOL + c0: kk * TCOL + c0 + cn],
                        start=(kk == 0), stop=(kk == 1)),
                        reads=wbt(BD2["bd_sj"] + kk * 256 + mi * 128, 128)
                        + ["st"], writes=[pt])
                vop(lambda e, mi=mi, pb=pb, c0=c0, cn=cn: e.tensor_add(
                    p_b[:, mi * TCOL + c0: mi * TCOL + c0 + cn],
                    p_b[:, mi * TCOL + c0: mi * TCOL + c0 + cn], pb),
                    reads=[pt, "p"], writes=["p"])

        load_wcp(0)
        load_wcp(1)

        for d in range(DELTA):
            dl, hi = d % 16, d // 16
            rot = d % 2
            pre_b = ar(O_PRE_B + rot * 1024, 1024).rearrange("p (h n) -> p h n", h=2)
            pre_bt = artok(O_PRE_B + rot * 1024, 1024)
            act_b = ar(O_ACT_B + rot * 1024, 1024).rearrange("p (h n) -> p h n", h=2)
            act_bt = artok(O_ACT_B + rot * 1024, 1024)
            pre_d = ar(O_PRE_D + rot * 1024, 1024).rearrange("p (h n) -> p h n", h=2)
            pre_dt = artok(O_PRE_D + rot * 1024, 1024)
            act_d = ar(O_ACT_D + rot * 1024, 1024).rearrange("p (h n) -> p h n", h=2)
            act_dt = artok(O_ACT_D + rot * 1024, 1024)
            vop(lambda e, d=d, pre_b=pre_b: e.tensor_add(
                pre_b, sh3(p_b, 31 + d, 512), sh3(bv_b, HALO, 512)),
                reads=["p", "bv"], writes=pre_bt)
            vop(lambda e, d=d, pre_d=pre_d: e.tensor_sub(
                pre_d, sh3(r_b, 31 + d, 512), sh3(r_b, HALO, 512)),
                reads=["r"], writes=pre_dt)
            aop(lambda e, d=d, pre_b=pre_b, act_b=act_b: e.activation(
                act_b, pre_b, AF.Gelu, bias=e_rep[:, d:d + 1]),
                reads=pre_bt + ["e_rep"], writes=act_bt)
            aop(lambda e, d=d, pre_d=pre_d, act_d=act_d: e.activation(
                act_d, pre_d, AF.Gelu, bias=bdmgr),
                reads=pre_dt + ["bdmgr"], writes=act_dt)
            for half in range(2):
                st_i = half * 2 + hi
                top(lambda e, dl=dl, half=half, st_i=st_i, act_b=act_b: e.matmul(
                    ps[st_i][:, 0:CH], mast_b[:, 15 - dl:143 - dl],
                    act_b[:, half, :], start=(dl == 0), stop=(dl == 15)),
                    reads=["mastb"] + act_bt, writes=[PT[st_i]])
                top(lambda e, dl=dl, half=half, st_i=st_i, act_d=act_d: e.matmul(
                    ps[4 + st_i][:, 0:CH], mast_d[:, 15 - dl:143 - dl],
                    act_d[:, half, :], start=(dl == 0), stop=(dl == 15)),
                    reads=["mastd"] + act_dt, writes=[PT[4 + st_i]])
        emit_sigs(c)

        for wi in range(2):
            vop(lambda e, wi=wi: e.memset(wtm2[wi], 0.0), writes=WTM_T2[wi])

        def shear_read(j, h, engine):
            reg = j % 2
            slot = (j % 2) * 5 + h % 5
            if slot < 2:
                wshv = ar(O_WSH + slot * PP, PP)
            elif slot < 6:
                wshv = ar(O_WSHX + (slot - 2) * PP, PP)
            else:
                wshv = ar(O_SQ + (slot - 6) * PP, PP)
            wsht = ["wsh%d" % slot]
            dma(engine, wshv,
                AP(scratch, reg * SREG + h * PP, [[SROW - 1, 128], [1, PP]]),
                reads=["scr%d" % reg], writes=wsht)
            return wshv, wsht

        def stage1(j):
            for i in (0, 2, 1, 3):
                top(lambda e, i=i, j=j: e.transpose(
                    ps[i][:, 0:128], ar(O_ESTK + i * CH + j * 128, 128), iden_s),
                    reads=artok(O_ESTK + i * CH + j * 128, 128) + ["iden"],
                    writes=[PT[i]])
            for i in range(4):
                half, ab = i // 2, i % 2
                vop(lambda e, i=i, half=half, ab=ab: e.reduce_sum(
                    sums[:, 16 * ab + half * 8:16 * ab + half * 8 + 8],
                    ps[i][:, 0:128].rearrange("p (h d) -> p h d", h=8), axis=AX.X),
                    reads=[PT[i]], writes=["sums"])
            vop(lambda e: e.tensor_add(sums[:, 32:48], sums[:, 0:16],
                                       sums[:, 16:32]),
                reads=["sums"], writes=["sums"])
            vop(lambda e: e.reciprocal(sums[:, 32:48], sums[:, 32:48]),
                reads=["sums"], writes=["sums"])
            wi = j % 2
            for i in range(4):
                half, ab = i // 2, i % 2
                dst = AP(sb["arena"], (O_WTM if wi == 0 else O_WTM1)
                         + half * 8 * PP + ab * 16,
                         [[ARENA, 128], [PP, 8], [1, 16]])
                rsl = sums[:, 32 + half * 8:32 + half * 8 + 8][:, :, None] \
                    .broadcast_to([128, 8, 16])
                vop(lambda e, i=i, dst=dst, rsl=rsl: e.tensor_mul(
                    dst, ps[i][:, 0:128].rearrange("p (h d) -> p h d", h=8), rsl),
                    reads=[PT[i], "sums"], writes=WTM_T2[wi])
            reg = j % 2
            dma("gpsimd", AP(scratch, reg * SREG + 97, [[SROW, 128], [1, SROW]]),
                wtm2[wi], reads=WTM_T2[wi] + ["sczero%d" % reg],
                writes=["scr%d" % reg])
            return {h: shear_read(j, h, "gpsimd") for h in range(5)}

        def banded(j, h):
            wblP = ar(O_WBL + (h % 2) * 256, 128)
            wblC = ar(O_WBL + (h % 2) * 256 + 128, 128)
            wblt = ["wbl%d" % (h % 2)]
            pair = h // 2
            obank = ps[6 + pair % 2]
            obt = PT[6 + pair % 2]
            orng = (h % 2) * 64
            top(lambda e, j=j, h=h, wblP=wblP, obank=obank, orng=orng: e.matmul(
                obank[orng:orng + 64, 0:128],
                val_b[:, j * C + h * 64:j * C + (h + 1) * 64], wblP,
                start=True, stop=False),
                reads=wblt + ["val%d" % j], writes=[obt])
            top(lambda e, j=j, h=h, wblC=wblC, obank=obank, orng=orng: e.matmul(
                obank[orng:orng + 64, 0:128],
                val_b[:, (1 + j) * C + h * 64:(1 + j) * C + (h + 1) * 64], wblC,
                start=False, stop=True),
                reads=wblt + ["val%d" % (1 + j)], writes=[obt])
            if h % 2 == 1:
                aop(lambda e, pair=pair, j=j, obank=obank: e.copy(
                    ar(O_ATT + pair * CH + j * 128, 128), obank[:, 0:128]),
                    reads=[obt], writes=artok(O_ATT + pair * CH + j * 128, 128))

        def stage2(j, rds):
            for h in range(NH):
                wshv, wsht = rds[h]
                pbank = ps[4 + (h % 2)]
                pbt = PT[4 + (h % 2)]
                top(lambda e, wshv=wshv, pbank=pbank: e.transpose(
                    pbank[:, 0:128], wshv[:, 0:128], iden_s),
                    reads=wsht + ["iden"], writes=[pbt])
                top(lambda e, wshv=wshv, pbank=pbank: e.transpose(
                    pbank[:, 128:256], wshv[:, 128:256], iden_s),
                    reads=wsht + ["iden"], writes=[pbt])
                if h + 5 < NH:
                    rds[h + 5] = shear_read(j, h + 5, "sync")
                wblP = ar(O_WBL + (h % 2) * 256, 128)
                wblC = ar(O_WBL + (h % 2) * 256 + 128, 128)
                wblt = ["wbl%d" % (h % 2)]
                aop(lambda e, wblP=wblP, pbank=pbank: e.copy(wblP, pbank[:, 0:128]),
                    reads=[pbt], writes=wblt)
                aop(lambda e, wblC=wblC, pbank=pbank: e.copy(wblC, pbank[:, 128:256]),
                    reads=[pbt], writes=wblt)
                if h > 0:
                    banded(j, h - 1)
            banded(j, NH - 1)

        pend = {}
        for j in range(NJT):
            pend[j] = stage1(j)
            if j > 0:
                stage2(j - 1, pend.pop(j - 1))
        stage2(NJT - 1, pend.pop(NJT - 1))

        for mo in range(KC):
            buf = mo % 2
            wdst = wbufA[:, buf * KC * 128:(buf + 1) * KC * 128]
            if mo >= 2:
                load_wcp(mo)
            pb = ps[mo % 2][:, 0:CH]
            pt = PT[mo % 2]
            for kk in range(KC):
                top(lambda e, kk=kk, pb=pb, wdst=wdst: e.matmul(
                    pb, wdst[:, kk * 128:(kk + 1) * 128], ar(O_ATT + kk * CH, CH),
                    start=(kk == 0), stop=(kk == KC - 1)),
                    reads=[wAtok(buf, kk // 2)] + artok(O_ATT + kk * CH, CH),
                    writes=[pt])
            vop(lambda e, mo=mo, pb=pb: e.tensor_scalar_add(
                x_k(mo), pb, bcpcol[:, mo:mo + 1]),
                reads=[pt, "bcpcol"], writes=[XT(mo)])
            vop(lambda e, mo=mo: e.tensor_add(x_k(mo), x_k(mo), x1_k(mo)),
                reads=[XT(mo), X1T(mo)], writes=[XT(mo)])
        load_w1(0)
        load_w1(1)

        def ffn2_evac(mo, pb, pt, c=c):
            slot = mo % 4
            sv = ar(O_STG4 + slot * CH, CH)
            stok = artok(O_STG4 + slot * CH, CH)
            vop(lambda e, mo=mo, pb=pb, sv=sv: e.tensor_scalar(
                sv, pb, b2col[:, mo:mo + 1], 0.5, AL.add, AL.mult),
                reads=[pt, "b2col"], writes=stok)
            vop(lambda e, mo=mo, sv=sv: e.tensor_add(sv, sv, x_k(mo)),
                reads=stok + [XT(mo)], writes=stok)
            dma("gpsimd" if slot % 2 else "sync",
                out_d[mo * 128:(mo + 1) * 128, c * CH:(c + 1) * CH],
                sv, reads=stok, writes=["outd%d_%d" % (c, mo)],
                always_inc=True,
                sem_key=("dg:out%d" if slot % 2 else "d:out%d") % slot)

        ffn_pass(x_k, lambda kk: [XT(kk)], CH, False, ffn2_evac, skip_w1=2,
                 after_w1=(lambda: (load_w1(0), load_w1(1)))
                 if c < NCHUNK - 1 else None)

    S.op("gpsimd", lambda e: e.engine_nop(),
         reads=["outd%d_%d" % (c, mo) for c in range(NCHUNK) for mo in range(KC)])

    S.finalize(nc, total_keys=("d:cst",))
    with nc.Block() as block:
        @block.tensor
        def _(eng):
            S.emit("tensor", eng)

        @block.vector
        def _(eng):
            S.emit("vector", eng)

        @block.scalar
        def _(eng):
            S.emit("scalar", eng)

        @block.gpsimd
        def _(eng):
            S.emit("gpsimd", eng)

        @block.sync
        def _(eng):
            S.emit("sync", eng)

    _CACHE["nc"] = nc
    _CACHE["stats"] = S.stats
    _CACHE["layout"] = dict(O_ESTK=O_ESTK, O_ATT=O_ATT, O_SIG=O_SIG,
                            O_WSH=O_WSH, O_WBL=O_WBL, O_STG=O_STG,
                            O_DISP=O_DISP, O_SQ=O_SQ)
    return nc


def kernel(**inputs):
    ii = {k: np.asarray(v) for k, v in inputs.items()}
    x = ii["L0_attn_x"]

    def blockdiag_re(w):
        bd = np.zeros((256, 256), np.float32)
        for h in range(NH):
            bd[h * BD:(h + 1) * BD, h * BD:(h + 1) * BD] = w
        return np.concatenate([bd[0:128, :], bd[128:256, :]], axis=1)

    def master(wout):
        m = np.zeros((128, 143), np.float32)
        for h in range(8):
            for k in range(BD):
                m[h * 16 + k, h * 16 + 15] = wout[k, 0]
        return m

    wd = ii["L0_attn_w_disp"]
    wdispT = np.zeros((128, KC * 256), np.float32)
    for kk in range(KC):
        wdispT[:, kk * 256:(kk + 1) * 256] = wd[kk * 128:(kk + 1) * 128, :]

    common = {
        "w1re": np.ascontiguousarray(
            ii["L0_ffn_W1"].reshape(KC, 128, MT1, 128).transpose(2, 1, 0, 3)
            .reshape(MT1, 128, KC * 128)),
        "w2": ii["L0_ffn_W2"],
        "wval": ii["L0_attn_w_val"],
        "wcpre": np.ascontiguousarray(
            ii["L0_attn_w_cproj"].reshape(KC, 128, KC, 128).transpose(2, 1, 0, 3)
            .reshape(KC, 128, KC * 128)),
        "wdispT": wdispT,
        "bd_state": blockdiag_re(ii["L0_attn_w_state"]),
        "bd_strain": blockdiag_re(ii["L0_attn_w_strain"]),
        "bd_damage": blockdiag_re(ii["L0_attn_w_damage"]),
        "bd_si": blockdiag_re(ii["L0_attn_w_state_i"]),
        "bd_sj": blockdiag_re(ii["L0_attn_w_state_j"]),
        "master_bond": master(ii["L0_attn_w_bond_out"]),
        "master_dmg": master(ii["L0_attn_w_damage_out"]),
        "relpeT": np.ascontiguousarray(ii["L0_attn_rel_pos_emb"].T),
        "wpos": ii["L0_attn_w_pos"],
        "theta_t": np.ascontiguousarray(ii["L0_ffn_quantum_phases"].reshape(32, 128).T),
        "wreal_t": np.ascontiguousarray(
            ii["L0_ffn_quantum_weights_real"].reshape(32, 128).T),
        "wimag_t": np.ascontiguousarray(
            ii["L0_ffn_quantum_weights_imag"].reshape(32, 128).T),
        "b1_t": np.ascontiguousarray(ii["L0_ffn_b1"].reshape(32, 128).T),
        "b2_t": np.ascontiguousarray(ii["L0_ffn_b2"].reshape(8, 128).T),
        "bdisp_t": np.ascontiguousarray(ii["L0_attn_b_disp"].reshape(2, 128).T),
        "bval_rep": np.broadcast_to(ii["L0_attn_b_val"], (128, C)),
        "bcproj_t": np.ascontiguousarray(ii["L0_attn_b_cproj"].reshape(8, 128).T),
        "bstate_rep": np.tile(ii["L0_attn_b_state"], 8).reshape(128, 1),
        "bstrain_rep": np.tile(ii["L0_attn_b_strain"], 8).reshape(128, 1),
        "bdmg_rep": np.tile(ii["L0_attn_b_damage"], 8).reshape(128, 1),
        "bdmgout_rep": np.broadcast_to(
            ii["L0_attn_b_damage_out"].reshape(1, 1), (128, 1)),
        "iden": np.eye(128, dtype=np.float32),
        "ones_col": np.ones((128, 1), np.float32),
        "hpi_col": np.full((128, 1), np.pi / 2, np.float32),
    }
    common = {k: np.ascontiguousarray(v, dtype=np.float32) for k, v in common.items()}

    in_maps = []
    for core in range(NCORE):
        b, kchunk = core // 4, core % 4
        start = kchunk * TC
        if kchunk == 0:
            xs = np.concatenate(
                [np.zeros((128, C), np.float32), x[b, 0:TC, :]], axis=0)
        else:
            xs = x[b, start - 128:start + TC, :]
        m = dict(common)
        m["xT"] = np.ascontiguousarray(xs.T, dtype=np.float32)
        qm = np.ones((128, HALO), np.float32)
        em = np.ones((128, 62), np.float32)
        inv = np.full((NCHUNK, 128, TCOL - 31), 1.0 / DELTA, np.float32)
        if kchunk == 0:
            qm[:] = 0.0
            dl = np.arange(128) % 16
            tt = np.arange(31)
            for ab in range(2):
                dd = ab * 16 + dl
                em[:, ab * 31:(ab + 1) * 31] = (
                    dd[:, None] >= 31 - tt[None, :]).astype(np.float32)
            tok = np.arange(-31, CH)
            cnt = np.clip(tok + 1, 1, DELTA).astype(np.float32)
            cnt[tok < 0] = DELTA
            inv[0, :, :] = 1.0 / cnt[None, :]
        m["qmask"] = qm
        m["expmask"] = em
        m["invc"] = np.ascontiguousarray(inv)
        in_maps.append(m)

    nc = build_nc()
    res = run_bass_kernel_spmd(nc, in_maps, core_ids=list(range(NCORE)))
    out = np.empty((B, T, C), np.float32)
    for core in range(NCORE):
        b, kchunk = core // 4, core % 4
        out[b, kchunk * TC:(kchunk + 1) * TC, :] = res.results[core]["out"].T
    return out
